# version 17
# baseline (speedup 1.0000x reference)
import numpy as np
import ml_dtypes
import concourse.bass as bass
import concourse.mybir as mybir
from concourse.bass_utils import run_bass_kernel_spmd
from concourse.aot_env import allow_ici_all_to_all

F32 = mybir.dt.float32
BF16 = mybir.dt.bfloat16
I32 = mybir.dt.int32
AF = mybir.ActivationFunctionType
ALU = mybir.AluOpType
AX = mybir.AxisListType

NF4 = np.array([-1.0, -0.6961928009986877, -0.5250730514526367, -0.39491748809814453,
                -0.28444138169288635, -0.18477343022823334, -0.09105003625154495, 0.0,
                0.07958029955625534, 0.16093020141124725, 0.24611230194568634,
                0.33791524171829224, 0.44070982933044434, 0.5626170039176941,
                0.7229568362236023, 1.0], dtype=np.float32)

EXP_SHIFT = 12.0
MASK_VAL = -1.0e9
LN_EPS = 1e-5


class CFG:
    NCORES = 8
    S = 8192
    D = 4096
    ROT = 64
    NI_CH = 11
    I_COUNTS = [11, 11, 11, 11, 11, 11, 10, 10]

    @property
    def TPC(self):
        return self.S // self.NCORES

    @property
    def NPIECE(self):
        return self.TPC // 128

    @property
    def NTT(self):
        return self.S // 512

    @property
    def NQT(self):
        return self.S // 512

    @property
    def DCH(self):
        return self.D // 128

    @property
    def NI(self):
        return self.NI_CH * 128


class SemC:
    __slots__ = ("h", "n")

    def __init__(self, h):
        self.h = h
        self.n = 0


class Plan:
    ENGINES = ("tensor", "vector", "scalar", "gpsimd", "sync")

    def __init__(self, nc):
        self.nc = nc
        self.cmds = {e: [] for e in self.ENGINES}
        self._lastw = {e: {} for e in self.ENGINES}
        self.sems = []

    def sem(self, name):
        s = SemC(self.nc.alloc_semaphore(name))
        self.sems.append(s)
        return s

    def wait(self, eng, sem, val=None):
        v = sem.n if val is None else val
        if v <= self._lastw[eng].get(id(sem), -1):
            return
        self._lastw[eng][id(sem)] = v
        self.cmds[eng].append(lambda e, s=sem, v=v: e.wait_ge(s.h, v))

    def op(self, eng, build, sem=None, d=1):
        if sem is not None:
            sem.n += d

        def fn(e, build=build, sem=sem, d=d):
            ins = build(e)
            if sem is not None:
                ins.then_inc(sem.h, d)
        self.cmds[eng].append(fn)

    def dma(self, eng, out_ap, in_ap, sem, transpose=False):
        if transpose:
            self.op(eng, lambda e, o=out_ap, i=in_ap: e.dma_start(out=o, in_=i, transpose=True), sem, 16)
        else:
            self.op(eng, lambda e, o=out_ap, i=in_ap: e.dma_start(out=o, in_=i), sem, 16)

    def barrier(self, engines=None):
        for e in (engines or self.ENGINES):
            for s in self.sems:
                self.wait(e, s)


def make_consts(cfg):
    c = {}
    c["ident_bf"] = np.eye(128, dtype=ml_dtypes.bfloat16)
    c["ident_f32"] = np.eye(128, dtype=np.float32)
    c["ones_bf"] = np.ones((128, 1), dtype=ml_dtypes.bfloat16)
    c["cvals"] = [0.0, 1.0, 1.0 / 16.0, -EXP_SHIFT, LN_EPS]
    c["consts"] = np.tile(np.array(c["cvals"], dtype=np.float32)[None, :], (128, 1))
    masks = np.zeros((128, 4, 512), dtype=np.float32)
    kk = np.arange(128)[:, None]
    qq = np.arange(512)[None, :]
    for j, off in enumerate((0, 128, 256, 384)):
        masks[:, j, :] = np.where(kk + off > qq, MASK_VAL, 0.0).astype(np.float32)
    c["masks"] = masks.reshape(128, 4 * 512)
    j = np.arange(0, cfg.ROT, 2, dtype=np.float32) / cfg.ROT
    inv_freq = (1.0 / 10000.0 ** j).astype(np.float32)
    t = np.arange(cfg.S, dtype=np.float32)
    ang = np.einsum("i,j->ij", t, inv_freq).astype(np.float32)
    c["sin"] = np.sin(ang).astype(np.float32)
    c["cos"] = np.cos(ang).astype(np.float32)
    return c


def build_nc(cfg, debug_taps=()):
    nc = bass.Bass("TRN2", target_bir_lowering=False, debug=False)
    S, D, TPC = cfg.S, cfg.D, cfg.TPC
    DCH, NI, NIC, NC = cfg.DCH, cfg.NI, cfg.NI_CH, cfg.NCORES
    ROTH = cfg.ROT // 2
    NT2 = TPC // 512
    assert TPC >= 512 and TPC % 512 == 0

    x_in = nc.dram_tensor("x", [TPC, D], F32, kind="ExternalInput").ap()
    wq_h = nc.dram_tensor("wq", [D, 512], F32, kind="ExternalInput").ap()
    wk_h = nc.dram_tensor("wk", [D, 512], F32, kind="ExternalInput").ap()
    wv_h = nc.dram_tensor("wv", [D, 512], F32, kind="ExternalInput").ap()
    wo_h = nc.dram_tensor("wo", [D, D], F32, kind="ExternalInput").ap()
    gc_h = nc.dram_tensor("gc", [NI, D], I32, kind="ExternalInput").ap()
    gs_h = nc.dram_tensor("gs", [NI, D // 64], F32, kind="ExternalInput").ap()
    uc_h = nc.dram_tensor("uc", [NI, D], I32, kind="ExternalInput").ap()
    us_h = nc.dram_tensor("us", [NI, D // 64], F32, kind="ExternalInput").ap()
    dc_h = nc.dram_tensor("dc", [D, NI], I32, kind="ExternalInput").ap()
    ds_h = nc.dram_tensor("ds", [D, NI // 64], F32, kind="ExternalInput").ap()
    out_h = nc.dram_tensor("out", [TPC, D], F32, kind="ExternalOutput").ap()

    cns = make_consts(cfg)
    ident_bf_d = nc.inline_tensor(cns["ident_bf"], "c_ident_bf")
    ident_f32_d = nc.inline_tensor(cns["ident_f32"], "c_ident_f32")
    ones_bf_d = nc.inline_tensor(np.asarray(cns["ones_bf"]), "c_ones_bf")
    consts_d = nc.inline_tensor(cns["consts"], "c_consts")
    masks_d = nc.inline_tensor(cns["masks"], "c_masks")
    cos_d = nc.inline_tensor(cns["cos"], "c_cos")
    sin_d = nc.inline_tensor(cns["sin"], "c_sin")

    NP = cfg.NPIECE
    ag0_in = nc.dram_tensor("ag0_in", [NP, D, 128], BF16)
    ag0_out = nc.dram_tensor("ag0_out", [NP // 2, NC, 2 * D, 128], BF16)
    q_sp = nc.dram_tensor("q_sp", [512, S], BF16)
    k_sp = nc.dram_tensor("k_sp", [512, S], BF16)
    v_sp = nc.dram_tensor("v_sp", [S, 512], BF16)
    a2a_in = [nc.dram_tensor("a2a_in0", [S, 256], BF16),
              nc.dram_tensor("a2a_in1", [S, 256], BF16)]
    a2a_out = [nc.dram_tensor("a2a_out0", [S, 256], BF16),
               nc.dram_tensor("a2a_out1", [S, 256], BF16)]
    y1_sp = nc.dram_tensor("y1_sp", [TPC, D], BF16)
    xp_sp = nc.dram_tensor("xp_sp", [TPC, D], F32)
    ag2_in = nc.dram_tensor("ag2_in", [NP, D, 128], BF16)
    ag2_out = nc.dram_tensor("ag2_out", [NP // 2, NC, 2 * D, 128], BF16)
    wg_d = nc.dram_tensor("wg_d", [NI, D], BF16)
    wu_d = nc.dram_tensor("wu_d", [NI, D], BF16)
    wd_d = nc.dram_tensor("wd_d", [D, NI], BF16)
    wgT = nc.dram_tensor("wgT", [D, NI], BF16)
    wuT = nc.dram_tensor("wuT", [D, NI], BF16)
    wdT = nc.dram_tensor("wdT", [NI, D], BF16)
    rs_in = nc.dram_tensor("rs_in", [NC * D, TPC], BF16)
    rs_out = nc.dram_tensor("rs_out", [D, TPC], BF16)

    rg = [list(range(NC))]

    SB_BASE = 17408

    def at(name, off, shape, dtype):
        return nc.alloc_sbuf_tensor_at(name, shape, dtype, offset=SB_BASE + off)

    consts_sb = at("consts_sb", 0, [128, 16], F32)
    ones_sb = at("ones_sb", 64, [128, 16], BF16)
    ident_bf = at("ident_bf_sb", 128, [128, 128], BF16)
    ident_f32 = at("ident_f32_sb", 384, [128, 128], F32)
    masks_sb = at("masks_sb", 896, [128, 4, 512], F32)
    stats_sb = at("stats_sb", 9088, [128, 64], F32)
    R1 = 10 * 1024
    R2 = R1 + 96 * 1024

    for i, v in enumerate(cns["cvals"]):
        nc.const_aps.register(F32, float(v), consts_sb[:, i:i + 1])

    wq_sb = at("wq_sb", R1, [128, DCH, 512], BF16)
    wk_sb = at("wk_sb", R1 + 32768, [128, DCH, 512], BF16)
    wv_sb = at("wv_sb", R1 + 65536, [128, DCH, 512], BF16)
    kh_sb = at("kh_sb", R1, [128, 2, S], BF16)
    vh_sb = at("vh_sb", R1 + 32768, [128, S // 128, 256], BF16)
    qt_sb = [at("qt_sb0", R1 + 65536, [128, 2, 512], BF16),
             at("qt_sb1", R1 + 67584, [128, 2, 512], BF16)]
    pT_sb = [at("pT_sb0", R1 + 69632, [128, 512], BF16),
             at("pT_sb1", R1 + 70656, [128, 512], BF16),
             at("pT_sb2", R1 + 96256, [128, 512], BF16)]
    scm_sb = at("scm_sb", R1 + 71680, [128, 512], F32)
    avs_sb = at("avs_sb", R1 + 73728, [128, 2, 512], BF16)
    ctb_sb = at("ctb_sb", R1 + 75776, [128, 4, 256], BF16)
    srow_sb = at("srow_sb", R1 + 77824, [128, 1024], F32)
    recs_sb = at("recs_sb", R1 + 81920, [128, 4], F32)
    avfm_sb = at("avfm_sb", R1, [128, DCH, TPC], BF16)
    avtm_sb = [at("avtm_sb0", R1 + 65536, [128, TPC // 128, 512], BF16),
               at("avtm_sb1", R1 + 73728, [128, TPC // 128, 512], BF16)]
    y2t_sb = at("y2t_sb", R1, [128, D], F32)
    xp_sb = at("xp_sb", R1 + 32768, [128, D], F32)
    y1t_sb = at("y1t_sb", R1 + 65536, [128, D], BF16)
    DQ = R2 + 69120
    _codes0 = at("codes_sb0", DQ, [128, 2048], I32)
    codes_sb = [_codes0, _codes0]
    acc_sb = at("acc_sb", DQ + 8192, [128, 2048], BF16)
    wrow_sb = at("wrow_sb", DQ + 12288, [128, 2048], BF16)
    scexp_sb = at("scexp_sb", DQ + 16384, [128, 2048], BF16)
    scl_sb = at("scl_sb", DQ + 20480, [128, 64], F32)
    wtst_sb = [at("wtst_sb0", R1 + 82432, [128, NI], BF16),
               at("wtst_sb1", R1 + 85248, [128, NI], BF16)]
    wdst_sb = [at("wdst_sb0", R1 + 88064, [128, D // 2], BF16),
               at("wdst_sb1", R1 + 92160, [128, D // 2], BF16)]
    lnx2_sb = at("lnx2_sb", R1, [128, DCH, TPC], BF16)
    dstg_sb = at("dstg_sb", R1, [128, DCH, TPC], BF16)
    inter_sb = at("inter_sb", R1 + 65536, [128, NIC, TPC], BF16)
    gsl_sb = at("gsl_sb", R1 + 65536 + NIC * TPC * 2, [128, 512], BF16)

    cos_sb = at("cos_sb", R2, [128, S // 128, ROTH], F32)
    sin_sb = at("sin_sb", R2 + (S // 128) * ROTH * 4, [128, S // 128, ROTH], F32)
    xt1_sb = at("xt1_sb", R2 + 16384, [128, D], F32)
    xsq1_sb = at("xsq1_sb", R2 + 32768, [128, D], F32)
    lnt1_sb = at("lnt1_sb", R2 + 49152, [128, D], BF16)
    lfm1_sb = at("lfm1_sb", R2 + 57344, [128, DCH, 128], BF16)
    lnx_sb = at("lnx_sb", R2 + 16384, [128, DCH, 512], BF16)
    qst_sb = at("qst_sb", R2 + 49152, [128, 4, 512], BF16)
    kst_sb = at("kst_sb", R2 + 53248, [128, 4, 512], BF16)
    vst_sb = at("vst_sb", R2 + 57344, [128, 4, 512], BF16)
    rin_sb = at("rin_sb", R2 + 61440, [128, 4, 64], BF16)
    rout_sb = at("rout_sb", R2 + 61952, [128, 4, 64], BF16)
    rtmp_sb = at("rtmp_sb", R2 + 62464, [128, 4, ROTH], F32)
    vtm_sb = at("vtm_sb", R2 + 63488, [128, 4, 4, 128], BF16)
    wo_sb = [at("wo_sb0", R2, [128, DCH, 256], BF16),
             at("wo_sb1", R2 + 16384, [128, DCH, 256], BF16)]
    y1f_sb = at("y1f_sb", R2 + 32768, [128, 2, TPC], BF16)
    y1s_sb = at("y1s_sb", R2 + 36864, [128, TPC // 128, 256], BF16)
    half1 = (NIC + 1) // 2
    whg_sb = at("whg_sb", R2, [128, DCH, half1 * 128], BF16)
    whu_sb = at("whu_sb", R2 + DCH * half1 * 128 * 2, [128, DCH, half1 * 128], BF16)
    wdt_sb = at("wdt_sb", R2, [128, NIC, D], BF16)
    fml2_sb = [at("fml2_sb0", R2, [128, DCH, 128], BF16),
               at("fml2_sb1", R2 + 8192, [128, DCH, 128], BF16)]

    pb = [nc.alloc_psum_tensor(f"pb{i}", [128, 512], F32) for i in range(6)]
    tb = [nc.alloc_psum_tensor(f"tbk{i}", [128, 1024], BF16) for i in range(2)]

    p = Plan(nc)
    sLD = p.sem("sLD")
    sGL = p.sem("sGL")
    sPE = p.sem("sPE")
    sV = p.sem("sV")
    sA = p.sem("sA")
    sCC = p.sem("sCC")
    sST = p.sem("sST")
    sG2 = p.sem("sG2")
    sC2 = p.sem("sC2")

    def t_mm(out, lhsT, rhs, start, stop, inc=True):
        p.op("tensor", lambda e, o=out, l=lhsT, r=rhs, s1=start, s2=stop:
             e.matmul(o, l, r, start=s1, stop=s2), sPE if inc else None)

    def t_tr(out, in_, ident=None, inc=True):
        idt = ident if ident is not None else ident_bf
        kdim = in_.shape[0]
        p.op("tensor", lambda e, o=out, i=in_, d=idt[0:kdim, 0:kdim]:
             e.transpose(o, i, d), sPE if inc else None)

    def v_op(build):
        p.op("vector", build, sV)

    def a_op(build):
        p.op("scalar", build, sA)

    def ln_stats(x_ap, fd):
        nst = fd // 512
        mean = stats_sb[:, 48:49]
        var = stats_sb[:, 49:50]
        sd = stats_sb[:, 50:51]
        rstd = stats_sb[:, 51:52]
        nmb = stats_sb[:, 52:53]
        for j in range(nst):
            v_op(lambda e, o=stats_sb[:, j * 6:(j + 1) * 6], i=x_ap[:, j * 512:(j + 1) * 512]:
                 e.bn_stats(o, i))
        p.wait("vector", sV)
        v_op(lambda e, o=stats_sb[:, 48:50], i=stats_sb[:, 0:nst * 6]: e.bn_aggr(o, i))
        p.wait("scalar", sV)
        a_op(lambda e, i=var, o=sd: e.activation(o, i, AF.Sqrt, bias=LN_EPS, scale=1.0))
        p.wait("vector", sA)
        v_op(lambda e, i=sd, o=rstd: e.reciprocal(o, i))
        p.wait("vector", sV)
        v_op(lambda e, o=nmb, a=mean, b=rstd: e.scalar_tensor_tensor(o, a, -1.0, b, ALU.mult, ALU.mult))
        p.wait("vector", sV)
        return rstd, nmb

    deq_tasks = []
    tr_tasks = {0: [], 1: [], 2: []}
    deq_st = {"v_done": 0, "store": {}, "cb": 0, "buf_guard": {0: 0, 1: 0}, "tpar": 0,
              "tr_guard": {}, "tq": 0}

    def _deq_chunk(codes_ap, scales_ap, dst_ap, fd, nblk, last_cb_of, mat):
        cell = {}

        def sub_dma():
            cb = deq_st["cb"]
            deq_st["cb"] ^= 1
            cell["cb"] = cb
            p.wait("scalar", sV, deq_st["buf_guard"][cb])
            p.dma("scalar", codes_sb[cb][:, 0:fd], codes_ap, sC2)
            p.dma("scalar", scl_sb[:, 0:nblk], scales_ap, sC2)
            cell["ready"] = sC2.n

        def mk_quad(k0):
            def sub(k0=k0):
                cb = cell["cb"]
                if k0 == 0:
                    p.wait("vector", sC2, cell["ready"])
                    p.wait("vector", sC2, deq_st.get("wrow_store", 0))
                for k in range(k0, k0 + 4):
                    if k == 0:
                        v_op(lambda e, o=acc_sb[:, 0:fd], i=codes_sb[cb][:, 0:fd],
                             kk=float(k), vv=float(NF4[k]):
                             e.tensor_scalar(o, i, kk, vv, ALU.is_equal, ALU.mult))
                    else:
                        v_op(lambda e, o=wrow_sb[:, 0:fd], i=codes_sb[cb][:, 0:fd],
                             kk=float(k), vv=float(NF4[k]):
                             e.tensor_scalar(o, i, kk, vv, ALU.is_equal, ALU.mult))
                        v_op(lambda e, o=acc_sb[:, 0:fd], a=acc_sb[:, 0:fd], b=wrow_sb[:, 0:fd]:
                             e.tensor_tensor(o, a, b, ALU.add))
            return sub

        def sub_fin():
            cb = cell["cb"]
            v_op(lambda e, o=scexp_sb[:, 0:fd].rearrange("p (b j) -> p b j", j=64),
                 i=scl_sb[:, 0:nblk].unsqueeze(2).broadcast_to([128, nblk, 64]): e.tensor_copy(o, i))
            p.wait("vector", sV)
            v_op(lambda e, o=wrow_sb[:, 0:fd], a=acc_sb[:, 0:fd], b=scexp_sb[:, 0:fd]:
                 e.tensor_tensor(o, a, b, ALU.mult))
            deq_st["buf_guard"][0] = sV.n
            deq_st["buf_guard"][1] = sV.n
            p.wait("scalar", sV)
            p.dma("scalar", dst_ap, wrow_sb[:, 0:fd], sC2)
            deq_st["wrow_store"] = sC2.n
            if last_cb_of is not None:
                deq_st["store"][last_cb_of] = sC2.n
        return [sub_dma, mk_quad(0), mk_quad(4), mk_quad(8), mk_quad(12), sub_fin]

    def _tr_pair(src_view, stage_pair, dst_view, mat, stage_key):
        def task():
            qi = 0
            q = "scalar"
            sem = sC2
            stage_sb = stage_pair[qi]
            p.wait(q, sC2, deq_st["store"][mat])
            gk = (qi, stage_key)
            p.wait(q, sem, deq_st["tr_guard"].get(gk, 0))
            p.dma(q, stage_sb, src_view, sem, transpose=True)
            p.wait(q, sem)
            p.dma(q, dst_view, stage_sb, sem)
            deq_st["tr_guard"][gk] = sem.n
        return task

    def build_deq_tasks():
        for mat, (codes_h_, scales_h_, dst_) in enumerate(
                ((gc_h, gs_h, wg_d), (uc_h, us_h, wu_d))):
            for ch in range(NIC):
                for hlf in range(2):
                    fd = 2048
                    last = mat if (ch == NIC - 1 and hlf == 1) else None
                    deq_tasks.extend(_deq_chunk(
                        codes_h_[ch * 128:(ch + 1) * 128, hlf * 2048:(hlf + 1) * 2048],
                        scales_h_[ch * 128:(ch + 1) * 128, hlf * 32:(hlf + 1) * 32],
                        dst_.ap()[ch * 128:(ch + 1) * 128, hlf * 2048:(hlf + 1) * 2048],
                        fd, 32, last, mat))
        for ch in range(DCH):
            last = 2 if ch == DCH - 1 else None
            deq_tasks.extend(_deq_chunk(
                dc_h[ch * 128:(ch + 1) * 128, :], ds_h[ch * 128:(ch + 1) * 128, :],
                wd_d.ap()[ch * 128:(ch + 1) * 128, :], NI, NI // 64, last, 2))
        wtst_pair = [wtst_sb[0][:, :], wtst_sb[1][:, :]]
        wdst_pair = [wdst_sb[0][:, :], wdst_sb[1][:, :]]
        for mat, (src, dst) in enumerate(((wg_d, wgT), (wu_d, wuT))):
            for hc in range(DCH):
                tr_tasks[mat].append(_tr_pair(src.ap()[:, hc * 128:(hc + 1) * 128],
                                              wtst_pair,
                                              dst.ap()[hc * 128:(hc + 1) * 128, :], mat, 0))
        for ic in range(NIC):
            for hf in range(2):
                tr_tasks[2].append(_tr_pair(
                    wd_d.ap()[hf * (D // 2):(hf + 1) * (D // 2), ic * 128:(ic + 1) * 128],
                    wdst_pair,
                    wdT.ap()[ic * 128:(ic + 1) * 128, hf * (D // 2):(hf + 1) * (D // 2)],
                    2, "wd"))

    def drip(n, trn=0):
        for _ in range(n):
            if deq_tasks:
                deq_tasks.pop(0)()
        for _ in range(trn):
            for mat in (0, 1, 2):
                if mat in deq_st["store"] and tr_tasks[mat]:
                    tr_tasks[mat].pop(0)()
                    break

    def tr_batch8(tiles, dst_flat, bank_idx, wait_v=None):
        tbk = tb[bank_idx]
        p.wait("tensor", sV, wait_v if wait_v is not None else sV.n)
        for j, t_ap in enumerate(tiles):
            t_tr(tbk[:, j * 128:(j + 1) * 128], t_ap, inc=(j == len(tiles) - 1))
        p.wait("vector", sPE)
        v_op(lambda e, o=dst_flat, i=tbk[:, 0:len(tiles) * 128]: e.tensor_copy(o, i))

    p.dma("sync", consts_sb[:, 0:len(cns["cvals"])], consts_d.ap(), sLD)
    p.dma("sync", ones_sb[:, 0:1], ones_bf_d.ap(), sLD)
    p.dma("sync", ident_bf[:, :], ident_bf_d.ap(), sLD)
    p.dma("sync", ident_f32[:, :], ident_f32_d.ap(), sLD)
    p.dma("sync", masks_sb[:, :, :], masks_d.ap().rearrange("p (j q) -> p j q", j=4), sLD)
    p.dma("sync", cos_sb[:, :, :], cos_d.ap().rearrange("(j p) f -> p j f", p=128), sLD)
    p.dma("sync", sin_sb[:, :, :], sin_d.ap().rearrange("(j p) f -> p j f", p=128), sLD)
    for w_sb, w_h in ((wq_sb, wq_h), (wk_sb, wk_h), (wv_sb, wv_h)):
        p.dma("gpsimd", w_sb[:, :, :], w_h.rearrange("(c p) o -> p c o", p=128), sGL)
    ph0_ld = sLD.n
    build_deq_tasks()
    p.wait("scalar", sLD, ph0_ld)
    a_op(lambda e: e.activation(stats_sb[:, 8:9], consts_sb[:, 0:1], AF.Exp))
    a_op(lambda e: e.activation(stats_sb[:, 8:9], consts_sb[:, 0:1], AF.Silu))

    ag0_base = sCC.n
    for pc in range(cfg.NPIECE):
        p.wait("sync", sLD, ph0_ld)
        p.wait("sync", sV)
        p.wait("sync", sA)
        p.dma("sync", xt1_sb[:, :], x_in[pc * 128:(pc + 1) * 128, :], sLD)
        p.wait("scalar", sLD)
        p.wait("vector", sLD)
        rstd, nmb = ln_stats(xt1_sb[:, :], D)
        p.wait("vector", sA)
        p.wait("vector", sST)
        v_op(lambda e, o=lnt1_sb[:, :], i=xt1_sb[:, :], a=rstd, b=nmb:
             e.tensor_scalar(o, i, a, b, ALU.mult, ALU.add))
        for b in range(DCH // 8):
            tiles = [lnt1_sb[:, (b * 8 + j) * 128:(b * 8 + j + 1) * 128] for j in range(8)]
            tr_batch8(tiles, lfm1_sb[:, b * 8:(b + 1) * 8, :].rearrange("p c t -> p (c t)"), b % 2)
        p.wait("sync", sV)
        p.dma("sync", ag0_in.ap()[pc].rearrange("(c p) t -> p c t", p=128),
              lfm1_sb[:, :, :], sST)
        if pc % 2 == 1:
            p.wait("gpsimd", sST)
            p.op("gpsimd", lambda e, pc=pc: e.collective_compute(
                "AllGather", ALU.bypass, replica_groups=rg,
                ins=[ag0_in.ap()[pc - 1:pc + 1].opt()],
                outs=[ag0_out.ap()[pc // 2].opt()]), sCC)

    p.wait("tensor", sGL)
    heads_rot = ((0, 0), (1, 2))
    bank_ev = {}
    for tt in range(cfg.NTT):
        r, c0 = (tt * 512) // TPC, (tt * 512) % TPC
        p.wait("sync", sCC, ag0_base + c0 // 256 + 2)
        p.wait("sync", sPE)
        p.wait("sync", sV)
        p.wait("sync", sA)
        for a_ in range(4):
            pgl = c0 // 128 + a_
            p.dma("sync", lnx_sb[:, :, a_ * 128:(a_ + 1) * 128],
                  ag0_out.ap()[pgl // 2, r, (pgl % 2) * D:(pgl % 2) * D + D, :]
                  .rearrange("(c p) t -> p c t", p=128), sLD)
        p.wait("tensor", sLD)
        p.wait("vector", sST)
        for proj, (w_sb, st_sb) in enumerate(((wq_sb, qst_sb), (wk_sb, kst_sb), (wv_sb, vst_sb))):
            for m in range(4):
                bi = (proj * 4 + m) % 6
                bank = pb[bi]
                p.wait("tensor", sV, bank_ev.get(bi, 0))
                for k in range(DCH):
                    t_mm(bank[:, :], w_sb[:, k, m * 128:(m + 1) * 128], lnx_sb[:, k, :],
                         start=(k == 0), stop=(k == DCH - 1), inc=(k == DCH - 1))
                p.wait("vector", sPE)
                v_op(lambda e, o=st_sb[:, m, :], i=bank[:, :]: e.tensor_copy(o, i))
                bank_ev[bi] = sV.n
                if (proj * 4 + m) % 2 == 0:
                    drip(1)
        st_ev = sV.n
        for st_sb in (qst_sb, kst_sb):
            for (e_h, m) in heads_rot:
                p.wait("tensor", sV, st_ev)
                for sub in range(4):
                    t_tr(tb[0][:, sub * 64:(sub + 1) * 64],
                         st_sb[0:64, m, sub * 128:(sub + 1) * 128], inc=(sub == 3))
                p.wait("vector", sPE)
                v_op(lambda e, o=rin_sb[:, :, :].rearrange("p c t -> p (c t)"), i=tb[0][:, 0:256]:
                     e.tensor_copy(o, i))
                cs = cos_sb[:, tt * 4:(tt + 1) * 4, :]
                sn = sin_sb[:, tt * 4:(tt + 1) * 4, :]
                ri = rin_sb[:, :, :].rearrange("p c (i two) -> p c i two", two=2)
                ro = rout_sb[:, :, :].rearrange("p c (i two) -> p c i two", two=2)
                ev, od = ri[:, :, :, 0], ri[:, :, :, 1]
                oe, oo = ro[:, :, :, 0], ro[:, :, :, 1]
                rt = rtmp_sb[:, :, :]
                v_op(lambda e, o=oe, a=ev, b=cs: e.tensor_tensor(o, a, b, ALU.mult))
                v_op(lambda e, o=rt, a=od, b=sn: e.tensor_tensor(o, a, b, ALU.mult))
                p.wait("vector", sV)
                v_op(lambda e, o=oe, a=oe, b=rt: e.tensor_tensor(o, a, b, ALU.add))
                v_op(lambda e, o=oo, a=od, b=cs: e.tensor_tensor(o, a, b, ALU.mult))
                p.wait("vector", sV)
                v_op(lambda e, o=rt, a=ev, b=sn: e.tensor_tensor(o, a, b, ALU.mult))
                p.wait("vector", sV)
                v_op(lambda e, o=oo, a=oo, b=rt: e.tensor_tensor(o, a, b, ALU.add))
                p.wait("vector", sV)
                p.wait("tensor", sV)
                for sub in range(4):
                    t_tr(tb[1][0:64, sub * 128:(sub + 1) * 128], rout_sb[:, sub, :], inc=(sub == 3))
                p.wait("vector", sPE)
                v_op(lambda e, o=st_sb[0:64, m, :], i=tb[1][0:64, 0:512]: e.tensor_copy(o, i))
                drip(1)
        for half in range(2):
            tiles = []
            for j in range(8):
                sub, m = half * 2 + j // 4, j % 4
                tiles.append(vst_sb[:, m, sub * 128:(sub + 1) * 128])
            tr_batch8(tiles, vtm_sb[:, half * 2:(half + 1) * 2, :, :].rearrange("p a m x -> p (a m x)"),
                      half, wait_v=st_ev)
        p.wait("sync", sV)
        p.dma("sync", q_sp.ap()[:, tt * 512:(tt + 1) * 512].rearrange("(m p) t -> p m t", p=128),
              qst_sb[:, :, :], sST)
        p.dma("sync", k_sp.ap()[:, tt * 512:(tt + 1) * 512].rearrange("(m p) t -> p m t", p=128),
              kst_sb[:, :, :], sST)
        p.dma("sync", v_sp.ap()[tt * 512:(tt + 1) * 512, :].rearrange("(s p) f -> p s f", p=128),
              vtm_sb[:, :, :, :].rearrange("p s m x -> p s (m x)"), sST)

    s_dbg = None
    if "sdbg" in debug_taps:
        s_dbg = nc.dram_tensor("dbg_srow", [2 * cfg.NQT, 1024], F32, kind="ExternalOutput").ap()
    for e_h in range(2):
        if e_h == 0:
            p.barrier()
        else:
            for eng in ("sync", "tensor", "vector", "scalar"):
                p.wait(eng, sPE)
                p.wait(eng, sV)
                p.wait(eng, sA)
        p.dma("sync", kh_sb[:, :, :],
              k_sp.ap()[e_h * 256:(e_h + 1) * 256, :].rearrange("(c p) t -> p c t", p=128), sLD)
        p.dma("sync", vh_sb[:, :, :],
              v_sp.ap()[:, e_h * 256:(e_h + 1) * 256].rearrange("(c p) f -> p c f", p=128), sLD)
        p.wait("tensor", sLD)
        probs_reuse = [0, 0, 0]
        score_reuse = [0, 0, 0]
        q_reuse = [0, 0]
        scm_used = [0, 0]
        scm_bufs = [scm_sb[:, :], srow_sb[:, 0:512]]
        recs_ev_prev = [0]
        sbank = {0: pb[0], 1: pb[1], 2: pb[5]}
        for qt in range(cfg.NQT):
            drip(7, trn=2)
            qb = qt % 2
            p.wait("sync", sPE, q_reuse[qb])
            p.dma("sync", qt_sb[qb][:, :, :],
                  q_sp.ap()[e_h * 256:(e_h + 1) * 256, qt * 512:(qt + 1) * 512]
                  .rearrange("(c p) t -> p c t", p=128), sLD)
            p.wait("tensor", sLD)
            n_kc = 4 * qt + 4
            exp_cnt = [0] * n_kc

            def emit_scores(kc):
                par = kc % 3
                bank = sbank[par]
                if score_reuse[par]:
                    p.wait("tensor", sA, score_reuse[par])
                t_mm(bank[:, :], kh_sb[:, 0, kc * 128:(kc + 1) * 128], qt_sb[qb][:, 0, :],
                     start=True, stop=False, inc=False)
                t_mm(bank[:, :], kh_sb[:, 1, kc * 128:(kc + 1) * 128], qt_sb[qb][:, 1, :],
                     start=False, stop=True, inc=True)
                pe_after = sPE.n
                if probs_reuse[par]:
                    p.wait("scalar", sPE, probs_reuse[par])
                if kc >= 4 * qt:
                    off = kc - 4 * qt
                    sc2 = scm_bufs[kc % 2]
                    p.wait("vector", sPE, pe_after)
                    p.wait("vector", sA, scm_used[kc % 2])
                    v_op(lambda e, o=sc2, a=bank[:, :], b=masks_sb[:, off, :]:
                         e.tensor_tensor(o, a, b, ALU.add))
                    p.wait("scalar", sV)
                    a_op(lambda e, o=pT_sb[par][:, :], i=sc2:
                         e.activation(o, i, AF.Exp, bias=-EXP_SHIFT, scale=1.0 / 16.0))
                    scm_used[kc % 2] = sA.n
                else:
                    p.wait("scalar", sPE, pe_after)
                    a_op(lambda e, o=pT_sb[par][:, :], i=bank[:, :]:
                         e.activation(o, i, AF.Exp, bias=-EXP_SHIFT, scale=1.0 / 16.0))
                score_reuse[par] = sA.n
                exp_cnt[kc] = sA.n

            def emit_av(kc):
                par = kc % 3
                if kc == 0:
                    p.wait("tensor", sV, recs_ev_prev[0])
                p.wait("tensor", sA, exp_cnt[kc])
                st = (kc == 0)
                sp = (kc == n_kc - 1)
                t_mm(pb[2][:, :], vh_sb[:, kc, 0:128], pT_sb[par][:, :], start=st, stop=sp, inc=False)
                t_mm(pb[3][:, :], vh_sb[:, kc, 128:256], pT_sb[par][:, :], start=st, stop=sp, inc=False)
                t_mm(pb[4][0:1, :], ones_sb[:, 0:1], pT_sb[par][:, :], start=st, stop=sp, inc=True)
                probs_reuse[par] = sPE.n

            emit_scores(0)
            emit_scores(1)
            for kc in range(2, n_kc):
                emit_scores(kc)
                emit_av(kc - 2)
            emit_av(n_kc - 2)
            emit_av(n_kc - 1)
            q_reuse[qb] = sPE.n
            p.wait("vector", sPE)
            p.wait("vector", sA)
            p.wait("vector", sST)
            v_op(lambda e, o=avs_sb[:, 0, :], i=pb[2][:, :]: e.tensor_copy(o, i))
            v_op(lambda e, o=avs_sb[:, 1, :], i=pb[3][:, :]: e.tensor_copy(o, i))
            avs_ev = sV.n
            v_op(lambda e, o=srow_sb[0:1, 0:512], i=pb[4][0:1, :]: e.tensor_copy(o, i))
            p.wait("vector", sV)
            v_op(lambda e, o=srow_sb[0:1, 512:1024], i=srow_sb[0:1, 0:512]: e.reciprocal(o, i))
            p.wait("vector", sV)
            p.wait("tensor", sV)
            for sub in range(4):
                t_tr(pb[2][:, sub:sub + 1], srow_sb[0:1, 512 + sub * 128:512 + (sub + 1) * 128],
                     ident=ident_f32, inc=(sub == 3))
            p.wait("vector", sPE)
            v_op(lambda e, o=recs_sb[:, 0:4], i=pb[2][:, 0:4]: e.tensor_copy(o, i))
            recs_ev_prev[0] = sV.n
            p.wait("vector", sV)
            tiles = []
            for j in range(8):
                sub, half = j // 2, j % 2
                tiles.append(avs_sb[:, half, sub * 128:(sub + 1) * 128])
            tbk = tb[0]
            p.wait("tensor", sV, avs_ev)
            for j, t_ap in enumerate(tiles):
                t_tr(tbk[:, j * 128:(j + 1) * 128], t_ap, inc=(j == 7))
            p.wait("vector", sPE)
            for sub in range(4):
                v_op(lambda e, o=ctb_sb[:, sub, :], i=tbk[:, sub * 256:(sub + 1) * 256],
                     r=recs_sb[:, sub:sub + 1]: e.tensor_scalar(o, i, r, None, ALU.mult))
            p.wait("sync", sV)
            p.dma("sync", a2a_in[e_h].ap()[qt * 512:(qt + 1) * 512, :]
                  .rearrange("(s p) f -> p s f", p=128), ctb_sb[:, :, :], sST)
            if s_dbg is not None:
                row = e_h * cfg.NQT + qt
                p.dma("sync", s_dbg[row:row + 1, :], srow_sb[0:1, 0:1024], sST)
        p.wait("gpsimd", sST)
        p.op("gpsimd", lambda e, e_h=e_h: e.collective_compute(
            "AllToAll", ALU.bypass, replica_groups=rg,
            ins=[a2a_in[e_h].ap().opt()], outs=[a2a_out[e_h].ap().opt()]), sCC)

    p.barrier()
    for r in range(NC):
        drip(4, trn=2)
        rb = r % 2
        p.wait("sync", sPE)
        for e2 in range(2):
            p.dma("sync", avtm_sb[rb][:, :, e2 * 256:(e2 + 1) * 256],
                  a2a_out[e2].ap()[r * TPC:(r + 1) * TPC, :].rearrange("(s p) f -> p s f", p=128), sLD)
        p.wait("tensor", sLD)
        n_t = TPC // 128
        for b in range(max(1, n_t // 2)):
            ts_a = b * 2
            n_ts = min(2, n_t - ts_a)
            tiles = []
            for dts in range(n_ts):
                for fsub in range(4):
                    tiles.append(avtm_sb[rb][:, ts_a + dts, fsub * 128:(fsub + 1) * 128])
            tbk = tb[b % 2]
            p.wait("tensor", sV)
            for j, t_ap in enumerate(tiles):
                t_tr(tbk[:, j * 128:(j + 1) * 128], t_ap, inc=(j == len(tiles) - 1))
            p.wait("vector", sPE)
            for dts in range(n_ts):
                v_op(lambda e, i=tbk[:, dts * 512:(dts + 1) * 512].rearrange("p (c t) -> p c t", c=4),
                     o=avfm_sb[:, r * 4:(r + 1) * 4, (ts_a + dts) * 128:(ts_a + dts + 1) * 128]:
                     e.tensor_copy(o, i))
    avfm_done = sV.n
    y1_ev = {}
    for g in range(D // 256):
        drip(4, trn=3)
        gb = g % 2
        p.wait("gpsimd", sPE)
        p.dma("gpsimd", wo_sb[gb][:, :, :],
              wo_h[:, g * 256:(g + 1) * 256].rearrange("(c p) o -> p c o", p=128), sGL)
        p.wait("tensor", sGL)
        p.wait("tensor", sV, max(avfm_done, y1_ev.get(g % 2, 0)))
        for oc in range(2):
            for k in range(DCH):
                for t2 in range(NT2):
                    t_mm(pb[oc * NT2 + t2][:, :], wo_sb[gb][:, k, oc * 128:(oc + 1) * 128],
                         avfm_sb[:, k, t2 * 512:(t2 + 1) * 512],
                         start=(k == 0), stop=(k == DCH - 1),
                         inc=(k == DCH - 1 and t2 == NT2 - 1))
        p.wait("vector", sPE)
        p.wait("vector", sST)
        for oc in range(2):
            for t2 in range(NT2):
                v_op(lambda e, o=y1f_sb[:, oc, t2 * 512:(t2 + 1) * 512], i=pb[oc * NT2 + t2][:, :]:
                     e.tensor_copy(o, i))
        y1_ev[g % 2] = sV.n
        y1f_ev = sV.n
        n_t = TPC // 128
        for b in range(max(1, n_t // 4)):
            ts0 = b * 4
            n_ts = min(4, n_t - ts0)
            tiles = []
            for dts in range(n_ts):
                for oc in range(2):
                    tiles.append(y1f_sb[:, oc, (ts0 + dts) * 128:(ts0 + dts + 1) * 128])
            tr_batch8(tiles, y1s_sb[:, ts0:ts0 + n_ts, :].rearrange("p s f -> p (s f)"), b % 2,
                      wait_v=y1f_ev)
        p.wait("sync", sV)
        p.dma("sync", y1_sp.ap()[:, g * 256:(g + 1) * 256].rearrange("(s p) f -> p s f", p=128),
              y1s_sb[:, :, :], sST)

    p.barrier()
    for pc in range(cfg.NPIECE):
        drip(8, trn=6)
        p.wait("sync", sV)
        p.wait("sync", sA)
        p.wait("sync", sST)
        p.dma("sync", y1t_sb[:, :], y1_sp.ap()[pc * 128:(pc + 1) * 128, :], sLD)
        p.dma("sync", xt1_sb[:, :], x_in[pc * 128:(pc + 1) * 128, :], sLD)
        p.wait("scalar", sLD)
        p.wait("vector", sLD)
        rstd, nmb = ln_stats(y1t_sb[:, :], D)
        p.wait("vector", sA)
        v_op(lambda e, o=xsq1_sb[:, :], i=y1t_sb[:, :], a=rstd, b=nmb:
             e.tensor_scalar(o, i, a, b, ALU.mult, ALU.add))
        p.wait("vector", sV)
        v_op(lambda e, o=xp_sb[:, :], a=xt1_sb[:, :], b=xsq1_sb[:, :]:
             e.tensor_tensor(o, a, b, ALU.add))
        p.wait("sync", sV)
        p.dma("sync", xp_sp.ap()[pc * 128:(pc + 1) * 128, :], xp_sb[:, :], sST)
        rstd, nmb = ln_stats(xp_sb[:, :], D)
        p.wait("vector", sA)
        p.wait("vector", sST)
        v_op(lambda e, o=lnt1_sb[:, :], i=xp_sb[:, :], a=rstd, b=nmb:
             e.tensor_scalar(o, i, a, b, ALU.mult, ALU.add))
        for b in range(DCH // 8):
            tiles = [lnt1_sb[:, (b * 8 + j) * 128:(b * 8 + j + 1) * 128] for j in range(8)]
            tr_batch8(tiles, lfm1_sb[:, b * 8:(b + 1) * 8, :].rearrange("p c t -> p (c t)"), b % 2)
        p.wait("sync", sV)
        p.dma("sync", ag2_in.ap()[pc].rearrange("(c p) t -> p c t", p=128),
              lfm1_sb[:, :, :], sST)
        if pc % 2 == 1:
            p.wait("gpsimd", sST)
            p.op("gpsimd", lambda e, pc=pc: e.collective_compute(
                "AllGather", ALU.bypass, replica_groups=rg,
                ins=[ag2_in.ap()[pc - 1:pc + 1].opt()],
                outs=[ag2_out.ap()[pc // 2].opt()]), sCC)

    while deq_tasks:
        deq_tasks.pop(0)()
    for mat in (0, 1, 2):
        while tr_tasks[mat]:
            tr_tasks[mat].pop(0)()

    p.barrier()
    halves = [list(range(0, half1)), list(range(half1, NIC))]
    for d_blk in range(NC):
        p.wait("sync", sPE)
        p.wait("sync", sV)
        p.wait("sync", sST)
        p.wait("vector", sPE)
        for a_ in range(NP):
            p.dma("sync", lnx2_sb[:, :, a_ * 128:(a_ + 1) * 128],
                  ag2_out.ap()[a_ // 2, d_blk, (a_ % 2) * D:(a_ % 2) * D + D, :]
                  .rearrange("(c p) t -> p c t", p=128), sLD)
        for half in halves:
            nm = len(half)
            m0 = half[0]
            p.wait("sync", sPE)
            p.dma("sync", whg_sb[:, :, 0:nm * 128],
                  wgT.ap()[:, m0 * 128:(m0 + nm) * 128].rearrange("(c p) i -> p c i", p=128), sLD)
            p.dma("sync", whu_sb[:, :, 0:nm * 128],
                  wuT.ap()[:, m0 * 128:(m0 + nm) * 128].rearrange("(c p) i -> p c i", p=128), sLD)
            p.wait("tensor", sLD)
            for ml in range(nm):
                m = m0 + ml
                p.wait("tensor", sV)
                p.wait("tensor", sA)
                for proj, wh in ((0, whg_sb), (1, whu_sb)):
                    for k in range(DCH):
                        for t2 in range(NT2):
                            t_mm(pb[proj * NT2 + t2][:, :], wh[:, k, ml * 128:(ml + 1) * 128],
                                 lnx2_sb[:, k, t2 * 512:(t2 + 1) * 512],
                                 start=(k == 0), stop=(k == DCH - 1),
                                 inc=(proj == 1 and k == DCH - 1 and t2 == NT2 - 1))
                for t2 in range(NT2):
                    p.wait("scalar", sPE)
                    p.wait("scalar", sV)
                    a_op(lambda e, o=gsl_sb[:, :], i=pb[t2][:, :]: e.activation(o, i, AF.Silu))
                    p.wait("vector", sA)
                    v_op(lambda e, o=inter_sb[:, m, t2 * 512:(t2 + 1) * 512], a=gsl_sb[:, :],
                         b=pb[NT2 + t2][:, :]: e.tensor_tensor(o, a, b, ALU.mult))
        p.wait("sync", sPE)
        p.dma("sync", wdt_sb[:, :, :], wdT.ap().rearrange("(c p) h -> p c h", p=128), sLD)
        p.wait("tensor", sLD)
        for hc in range(DCH):
            banks = [pb[4 + t2] if hc % 2 == 0 else pb[0 + t2] for t2 in range(NT2)]
            p.wait("tensor", sV)
            for m in range(NIC):
                for t2 in range(NT2):
                    t_mm(banks[t2][:, :], wdt_sb[:, m, hc * 128:(hc + 1) * 128],
                         inter_sb[:, m, t2 * 512:(t2 + 1) * 512],
                         start=(m == 0), stop=(m == NIC - 1),
                         inc=(m == NIC - 1 and t2 == NT2 - 1))
            p.wait("vector", sPE)
            p.wait("vector", sST)
            for t2 in range(NT2):
                v_op(lambda e, o=dstg_sb[:, hc, t2 * 512:(t2 + 1) * 512], i=banks[t2][:, :]:
                     e.tensor_copy(o, i))
        p.wait("sync", sV)
        p.dma("sync", rs_in.ap()[d_blk * D:(d_blk + 1) * D, :]
              .rearrange("(c p) t -> p c t", p=128), dstg_sb[:, :, :], sST)
    p.wait("gpsimd", sST)
    p.op("gpsimd", lambda e: e.collective_compute(
        "ReduceScatter", ALU.add, replica_groups=rg,
        ins=[rs_in.ap().opt()], outs=[rs_out.ap().opt()]), sCC)

    p.barrier()
    for pc in range(cfg.NPIECE):
        fb = pc % 2
        p.wait("sync", sPE)
        p.wait("sync", sV)
        p.wait("sync", sST)
        p.dma("sync", fml2_sb[fb][:, :, :],
              rs_out.ap()[:, pc * 128:(pc + 1) * 128].rearrange("(c p) t -> p c t", p=128), sLD)
        p.dma("sync", xp_sb[:, :], xp_sp.ap()[pc * 128:(pc + 1) * 128, :], sLD)
        p.wait("tensor", sLD)
        for b in range(DCH // 8):
            tiles = [fml2_sb[fb][:, b * 8 + j, :] for j in range(8)]
            tbk = tb[b % 2]
            p.wait("tensor", sV)
            for j, t_ap in enumerate(tiles):
                t_tr(tbk[:, j * 128:(j + 1) * 128], t_ap, inc=(j == 7))
            p.wait("vector", sPE)
            v_op(lambda e, o=y2t_sb[:, b * 8 * 128:(b + 1) * 8 * 128], i=tbk[:, 0:1024]:
                 e.tensor_copy(o, i))
        rstd, nmb = ln_stats(y2t_sb[:, :], D)
        p.wait("vector", sA)
        v_op(lambda e, o=xsq1_sb[:, :], i=y2t_sb[:, :], a=rstd, b=nmb:
             e.tensor_scalar(o, i, a, b, ALU.mult, ALU.add))
        p.wait("vector", sV)
        v_op(lambda e, o=xp_sb[:, :], a=xp_sb[:, :], b=xsq1_sb[:, :]:
             e.tensor_tensor(o, a, b, ALU.add))
        p.wait("sync", sV)
        p.dma("sync", out_h[pc * 128:(pc + 1) * 128, :], xp_sb[:, :], sST)
    p.wait("sync", sST)
    if debug_taps:
        scratch = {"ag0_in": ag0_in, "ag0_out": ag0_out, "q_sp": q_sp, "k_sp": k_sp,
                   "v_sp": v_sp, "a2a_in": a2a_in, "a2a_out": a2a_out, "y1_sp": y1_sp,
                   "xp_sp": xp_sp, "ag2_in": ag2_in, "ag2_out": ag2_out, "wg_d": wg_d,
                   "wgT": wgT, "wdT": wdT, "ar_in": rs_in, "ar_out": rs_out}
        for nm in debug_taps:
            if nm == "sdbg":
                continue
            t = scratch[nm]
            dbg = nc.dram_tensor("dbg_" + nm, list(t.ap().shape), t.ap().dtype,
                                 kind="ExternalOutput").ap()
            p.dma("sync", dbg, t.ap(), sST)
        p.wait("sync", sST)
    p.barrier()

    with nc.Block() as block:
        with allow_ici_all_to_all():
            for eng_name, attr in (("gpsimd", block.gpsimd), ("scalar", block.scalar),
                                   ("tensor", block.tensor), ("vector", block.vector),
                                   ("sync", block.sync)):
                def body(e, cmds=p.cmds[eng_name]):
                    for fn in cmds:
                        fn(e)
                attr(body)
    return nc



def shard_inputs(cfg, a):
    NC = cfg.NCORES
    x = np.asarray(a["L0_attn_hidden_states"], np.float32).reshape(cfg.S, cfg.D)
    wq = np.asarray(a["L0_attn_Wq"], np.float32)
    wk = np.asarray(a["L0_attn_Wk"], np.float32)
    wv = np.asarray(a["L0_attn_Wv"], np.float32)
    wo = np.asarray(a["L0_attn_Wo"], np.float32)
    gc = np.asarray(a["L0_ffn_gate_codes"], np.int32)
    gs = np.asarray(a["L0_ffn_gate_scales"], np.float32)
    uc = np.asarray(a["L0_ffn_up_codes"], np.int32)
    us = np.asarray(a["L0_ffn_up_scales"], np.float32)
    dc = np.asarray(a["L0_ffn_down_codes"], np.int32)
    ds = np.asarray(a["L0_ffn_down_scales"], np.float32)
    counts = cfg.I_COUNTS
    starts = np.cumsum([0] + counts[:-1]) * 128
    NI = cfg.NI
    in_maps = []
    for c in range(NC):
        s0 = int(starts[c])
        n = counts[c] * 128

        def padr(m):
            if m.shape[0] == NI:
                return m
            return np.concatenate(
                [m, np.zeros((NI - m.shape[0],) + m.shape[1:], m.dtype)], axis=0)

        def padc(m, cols):
            if m.shape[1] == cols:
                return m
            return np.concatenate(
                [m, np.zeros((m.shape[0], cols - m.shape[1]), m.dtype)], axis=1)

        in_maps.append({
            "x": np.ascontiguousarray(x[c * cfg.TPC:(c + 1) * cfg.TPC, :]),
            "wq": np.ascontiguousarray(wq[:, c * 512:(c + 1) * 512]),
            "wk": np.ascontiguousarray(wk[:, c * 512:(c + 1) * 512]),
            "wv": np.ascontiguousarray(wv[:, c * 512:(c + 1) * 512]),
            "wo": np.ascontiguousarray(wo),
            "gc": padr(np.ascontiguousarray(gc[s0:s0 + n, :])),
            "gs": padr(np.ascontiguousarray(gs[s0:s0 + n, :])),
            "uc": padr(np.ascontiguousarray(uc[s0:s0 + n, :])),
            "us": padr(np.ascontiguousarray(us[s0:s0 + n, :])),
            "dc": padc(np.ascontiguousarray(dc[:, s0:s0 + n]), NI),
            "ds": padc(np.ascontiguousarray(ds[:, s0 // 64:(s0 + n) // 64]), NI // 64),
        })
    return in_maps


def kernel(**inputs):
    cfg = CFG()
    in_maps = shard_inputs(cfg, inputs)
    nc = build_nc(cfg)
    res = run_bass_kernel_spmd(nc, in_maps, core_ids=list(range(cfg.NCORES)))
    out = np.concatenate([np.asarray(r["out"]) for r in res.results], axis=0)
    return out.reshape(1, cfg.S, cfg.D).astype(np.float32)


# version 18
# speedup vs baseline: 1.0340x; 1.0340x over previous
import numpy as np
import ml_dtypes
import concourse.bass as bass
import concourse.mybir as mybir
from concourse.bass_utils import run_bass_kernel_spmd
from concourse.aot_env import allow_ici_all_to_all

F32 = mybir.dt.float32
BF16 = mybir.dt.bfloat16
I32 = mybir.dt.int32
AF = mybir.ActivationFunctionType
ALU = mybir.AluOpType
AX = mybir.AxisListType

NF4 = np.array([-1.0, -0.6961928009986877, -0.5250730514526367, -0.39491748809814453,
                -0.28444138169288635, -0.18477343022823334, -0.09105003625154495, 0.0,
                0.07958029955625534, 0.16093020141124725, 0.24611230194568634,
                0.33791524171829224, 0.44070982933044434, 0.5626170039176941,
                0.7229568362236023, 1.0], dtype=np.float32)

EXP_SHIFT = 12.0
MASK_VAL = -1.0e9
LN_EPS = 1e-5


class CFG:
    NCORES = 8
    S = 8192
    D = 4096
    ROT = 64
    NI_CH = 11
    I_COUNTS = [11, 11, 11, 11, 11, 11, 10, 10]

    @property
    def TPC(self):
        return self.S // self.NCORES

    @property
    def NPIECE(self):
        return self.TPC // 128

    @property
    def NTT(self):
        return self.S // 512

    @property
    def NQT(self):
        return self.S // 512

    @property
    def DCH(self):
        return self.D // 128

    @property
    def NI(self):
        return self.NI_CH * 128


class SemC:
    __slots__ = ("h", "n")

    def __init__(self, h):
        self.h = h
        self.n = 0


class Plan:
    ENGINES = ("tensor", "vector", "scalar", "gpsimd", "sync")

    def __init__(self, nc):
        self.nc = nc
        self.cmds = {e: [] for e in self.ENGINES}
        self._lastw = {e: {} for e in self.ENGINES}
        self.sems = []

    def sem(self, name):
        s = SemC(self.nc.alloc_semaphore(name))
        self.sems.append(s)
        return s

    def wait(self, eng, sem, val=None):
        v = sem.n if val is None else val
        if v <= self._lastw[eng].get(id(sem), -1):
            return
        self._lastw[eng][id(sem)] = v
        self.cmds[eng].append(lambda e, s=sem, v=v: e.wait_ge(s.h, v))

    def op(self, eng, build, sem=None, d=1):
        if sem is not None:
            sem.n += d

        def fn(e, build=build, sem=sem, d=d):
            ins = build(e)
            if sem is not None:
                ins.then_inc(sem.h, d)
        self.cmds[eng].append(fn)

    def dma(self, eng, out_ap, in_ap, sem, transpose=False):
        if transpose:
            self.op(eng, lambda e, o=out_ap, i=in_ap: e.dma_start(out=o, in_=i, transpose=True), sem, 16)
        else:
            self.op(eng, lambda e, o=out_ap, i=in_ap: e.dma_start(out=o, in_=i), sem, 16)

    def barrier(self, engines=None):
        for e in (engines or self.ENGINES):
            for s in self.sems:
                self.wait(e, s)


def make_consts(cfg):
    c = {}
    c["ident_bf"] = np.eye(128, dtype=ml_dtypes.bfloat16)
    c["ident_f32"] = np.eye(128, dtype=np.float32)
    c["ones_bf"] = np.ones((128, 1), dtype=ml_dtypes.bfloat16)
    c["cvals"] = [0.0, 1.0, 1.0 / 16.0, -EXP_SHIFT, LN_EPS]
    c["consts"] = np.tile(np.array(c["cvals"], dtype=np.float32)[None, :], (128, 1))
    masks = np.zeros((128, 4, 512), dtype=np.float32)
    kk = np.arange(128)[:, None]
    qq = np.arange(512)[None, :]
    for j, off in enumerate((0, 128, 256, 384)):
        masks[:, j, :] = np.where(kk + off > qq, MASK_VAL, 0.0).astype(np.float32)
    c["masks"] = masks.reshape(128, 4 * 512)
    j = np.arange(0, cfg.ROT, 2, dtype=np.float32) / cfg.ROT
    inv_freq = (1.0 / 10000.0 ** j).astype(np.float32)
    t = np.arange(cfg.S, dtype=np.float32)
    ang = np.einsum("i,j->ij", t, inv_freq).astype(np.float32)
    c["sin"] = np.sin(ang).astype(np.float32)
    c["cos"] = np.cos(ang).astype(np.float32)
    return c


def build_nc(cfg, debug_taps=()):
    nc = bass.Bass("TRN2", target_bir_lowering=False, debug=False)
    S, D, TPC = cfg.S, cfg.D, cfg.TPC
    DCH, NI, NIC, NC = cfg.DCH, cfg.NI, cfg.NI_CH, cfg.NCORES
    ROTH = cfg.ROT // 2
    NT2 = TPC // 512
    assert TPC >= 512 and TPC % 512 == 0

    x_in = nc.dram_tensor("x", [TPC, D], F32, kind="ExternalInput").ap()
    wq_h = nc.dram_tensor("wq", [D, 512], F32, kind="ExternalInput").ap()
    wk_h = nc.dram_tensor("wk", [D, 512], F32, kind="ExternalInput").ap()
    wv_h = nc.dram_tensor("wv", [D, 512], F32, kind="ExternalInput").ap()
    wo_h = nc.dram_tensor("wo", [D, D], F32, kind="ExternalInput").ap()
    gc_h = nc.dram_tensor("gc", [NI, D], I32, kind="ExternalInput").ap()
    gs_h = nc.dram_tensor("gs", [NI, D // 64], F32, kind="ExternalInput").ap()
    uc_h = nc.dram_tensor("uc", [NI, D], I32, kind="ExternalInput").ap()
    us_h = nc.dram_tensor("us", [NI, D // 64], F32, kind="ExternalInput").ap()
    dc_h = nc.dram_tensor("dc", [D, NI], I32, kind="ExternalInput").ap()
    ds_h = nc.dram_tensor("ds", [D, NI // 64], F32, kind="ExternalInput").ap()
    out_h = nc.dram_tensor("out", [TPC, D], F32, kind="ExternalOutput").ap()

    cns = make_consts(cfg)
    ident_bf_d = nc.inline_tensor(cns["ident_bf"], "c_ident_bf")
    ident_f32_d = nc.inline_tensor(cns["ident_f32"], "c_ident_f32")
    ones_bf_d = nc.inline_tensor(np.asarray(cns["ones_bf"]), "c_ones_bf")
    consts_d = nc.inline_tensor(cns["consts"], "c_consts")
    masks_d = nc.inline_tensor(cns["masks"], "c_masks")
    cos_d = nc.inline_tensor(cns["cos"], "c_cos")
    sin_d = nc.inline_tensor(cns["sin"], "c_sin")

    NP = cfg.NPIECE
    ag0_in = nc.dram_tensor("ag0_in", [NP, D, 128], BF16)
    ag0_out = nc.dram_tensor("ag0_out", [NP // 2, NC, 2 * D, 128], BF16)
    q_sp = nc.dram_tensor("q_sp", [512, S], BF16)
    k_sp = nc.dram_tensor("k_sp", [512, S], BF16)
    v_sp = nc.dram_tensor("v_sp", [S, 512], BF16)
    a2a_in = [nc.dram_tensor("a2a_in0", [S, 256], BF16),
              nc.dram_tensor("a2a_in1", [S, 256], BF16)]
    a2a_out = [nc.dram_tensor("a2a_out0", [S, 256], BF16),
               nc.dram_tensor("a2a_out1", [S, 256], BF16)]
    y1_sp = nc.dram_tensor("y1_sp", [TPC, D], BF16)
    xp_sp = nc.dram_tensor("xp_sp", [TPC, D], F32)
    ag2_in = nc.dram_tensor("ag2_in", [NP, D, 128], BF16)
    ag2_out = nc.dram_tensor("ag2_out", [NP // 2, NC, 2 * D, 128], BF16)
    wg_d = nc.dram_tensor("wg_d", [NI, D], BF16)
    wu_d = nc.dram_tensor("wu_d", [NI, D], BF16)
    wd_d = nc.dram_tensor("wd_d", [D, NI], BF16)
    wgT = nc.dram_tensor("wgT", [D, NI], BF16)
    wuT = nc.dram_tensor("wuT", [D, NI], BF16)
    wdT = nc.dram_tensor("wdT", [NI, D], BF16)
    rs_in = nc.dram_tensor("rs_in", [NC * D, TPC], BF16)
    rs_out = nc.dram_tensor("rs_out", [D, TPC], BF16)

    rg = [list(range(NC))]

    SB_BASE = 17408

    def at(name, off, shape, dtype):
        return nc.alloc_sbuf_tensor_at(name, shape, dtype, offset=SB_BASE + off)

    consts_sb = at("consts_sb", 0, [128, 16], F32)
    ones_sb = at("ones_sb", 64, [128, 16], BF16)
    ident_bf = at("ident_bf_sb", 128, [128, 128], BF16)
    ident_f32 = at("ident_f32_sb", 384, [128, 128], F32)
    masks_sb = at("masks_sb", 896, [128, 4, 512], F32)
    stats_sb = at("stats_sb", 9088, [128, 64], F32)
    R1 = 10 * 1024
    R2 = R1 + 96 * 1024

    for i, v in enumerate(cns["cvals"]):
        nc.const_aps.register(F32, float(v), consts_sb[:, i:i + 1])

    wq_sb = at("wq_sb", R1, [128, DCH, 512], BF16)
    wk_sb = at("wk_sb", R1 + 32768, [128, DCH, 512], BF16)
    wv_sb = at("wv_sb", R1 + 65536, [128, DCH, 512], BF16)
    kh_sb = at("kh_sb", R1, [128, 2, S], BF16)
    vh_sb = at("vh_sb", R1 + 32768, [128, S // 128, 256], BF16)
    qt_sb = [at("qt_sb0", R1 + 65536, [128, 2, 512], BF16),
             at("qt_sb1", R1 + 67584, [128, 2, 512], BF16)]
    pT_sb = [at("pT_sb0", R1 + 69632, [128, 512], BF16),
             at("pT_sb1", R1 + 70656, [128, 512], BF16),
             at("pT_sb2", R1 + 96256, [128, 512], BF16)]
    scm_sb = at("scm_sb", R1 + 71680, [128, 512], F32)
    avs_sb = at("avs_sb", R1 + 73728, [128, 2, 512], BF16)
    ctb_sb = at("ctb_sb", R1 + 75776, [128, 4, 256], BF16)
    srow_sb = at("srow_sb", R1 + 77824, [128, 1024], F32)
    recs_sb = at("recs_sb", R1 + 81920, [128, 4], F32)
    avfm_sb = at("avfm_sb", R1, [128, DCH, TPC], BF16)
    avtm_sb = [at("avtm_sb0", R1 + 65536, [128, TPC // 128, 512], BF16),
               at("avtm_sb1", R1 + 73728, [128, TPC // 128, 512], BF16)]
    y2t_sb = at("y2t_sb", R1, [128, D], F32)
    xp_sb = at("xp_sb", R1 + 32768, [128, D], F32)
    y1t_sb = at("y1t_sb", R1 + 65536, [128, D], BF16)
    DQ = R2 + 69120
    _codes0 = at("codes_sb0", DQ, [128, 2048], I32)
    codes_sb = [_codes0, _codes0]
    acc_sb = at("acc_sb", DQ + 8192, [128, 2048], BF16)
    wrow_sb = at("wrow_sb", DQ + 12288, [128, 2048], BF16)
    scexp_sb = at("scexp_sb", DQ + 16384, [128, 2048], BF16)
    scl_sb = at("scl_sb", DQ + 20480, [128, 64], F32)
    wtst_sb = [at("wtst_sb0", R1 + 82432, [128, NI], BF16),
               at("wtst_sb1", R1 + 85248, [128, NI], BF16)]
    wdst_sb = [at("wdst_sb0", R1 + 88064, [128, D // 2], BF16),
               at("wdst_sb1", R1 + 92160, [128, D // 2], BF16)]
    lnx2_sb = at("lnx2_sb", R1, [128, DCH, TPC], BF16)
    dstg_sb = at("dstg_sb", R1, [128, DCH, TPC], BF16)
    inter_sb = at("inter_sb", R1 + 65536, [128, NIC, TPC], BF16)
    gsl_sb = at("gsl_sb", R1 + 65536 + NIC * TPC * 2, [128, 512], BF16)

    cos_sb = at("cos_sb", R2, [128, S // 128, ROTH], F32)
    sin_sb = at("sin_sb", R2 + (S // 128) * ROTH * 4, [128, S // 128, ROTH], F32)
    xt1_sb = at("xt1_sb", R2 + 16384, [128, D], F32)
    xsq1_sb = at("xsq1_sb", R2 + 32768, [128, D], F32)
    lnt1_sb = at("lnt1_sb", R2 + 49152, [128, D], BF16)
    lfm1_sb = at("lfm1_sb", R2 + 57344, [128, DCH, 128], BF16)
    lnx_sb = at("lnx_sb", R2 + 16384, [128, DCH, 512], BF16)
    qst_sb = at("qst_sb", R2 + 49152, [128, 4, 512], BF16)
    kst_sb = at("kst_sb", R2 + 53248, [128, 4, 512], BF16)
    vst_sb = at("vst_sb", R2 + 57344, [128, 4, 512], BF16)
    rin_sb = at("rin_sb", R2 + 61440, [128, 4, 64], BF16)
    rout_sb = at("rout_sb", R2 + 61952, [128, 4, 64], BF16)
    rtmp_sb = at("rtmp_sb", R2 + 62464, [128, 4, ROTH], F32)
    vtm_sb = at("vtm_sb", R2 + 63488, [128, 4, 4, 128], BF16)
    wo_sb = [at("wo_sb0", R2, [128, DCH, 256], BF16),
             at("wo_sb1", R2 + 16384, [128, DCH, 256], BF16)]
    y1f_sb = at("y1f_sb", R2 + 32768, [128, 2, TPC], BF16)
    y1s_sb = at("y1s_sb", R2 + 36864, [128, TPC // 128, 256], BF16)
    half1 = (NIC + 1) // 2
    whg_sb = at("whg_sb", R2, [128, DCH, half1 * 128], BF16)
    whu_sb = at("whu_sb", R2 + DCH * half1 * 128 * 2, [128, DCH, half1 * 128], BF16)
    wdt_sb = at("wdt_sb", R2, [128, NIC, D], BF16)
    fml2_sb = [at("fml2_sb0", R2, [128, DCH, 128], BF16),
               at("fml2_sb1", R2 + 8192, [128, DCH, 128], BF16)]

    pb = [nc.alloc_psum_tensor(f"pb{i}", [128, 512], F32) for i in range(6)]
    tb = [nc.alloc_psum_tensor(f"tbk{i}", [128, 1024], BF16) for i in range(2)]

    p = Plan(nc)
    sLD = p.sem("sLD")
    sGL = p.sem("sGL")
    sPE = p.sem("sPE")
    sV = p.sem("sV")
    sA = p.sem("sA")
    sCC = p.sem("sCC")
    sST = p.sem("sST")
    sG2 = p.sem("sG2")
    sC2 = p.sem("sC2")

    def t_mm(out, lhsT, rhs, start, stop, inc=True):
        p.op("tensor", lambda e, o=out, l=lhsT, r=rhs, s1=start, s2=stop:
             e.matmul(o, l, r, start=s1, stop=s2), sPE if inc else None)

    def t_tr(out, in_, ident=None, inc=True):
        idt = ident if ident is not None else ident_bf
        kdim = in_.shape[0]
        p.op("tensor", lambda e, o=out, i=in_, d=idt[0:kdim, 0:kdim]:
             e.transpose(o, i, d), sPE if inc else None)

    def v_op(build):
        p.op("vector", build, sV)

    def a_op(build):
        p.op("scalar", build, sA)

    def ln_stats(x_ap, fd):
        nst = fd // 512
        mean = stats_sb[:, 48:49]
        var = stats_sb[:, 49:50]
        sd = stats_sb[:, 50:51]
        rstd = stats_sb[:, 51:52]
        nmb = stats_sb[:, 52:53]
        for j in range(nst):
            v_op(lambda e, o=stats_sb[:, j * 6:(j + 1) * 6], i=x_ap[:, j * 512:(j + 1) * 512]:
                 e.bn_stats(o, i))
        p.wait("vector", sV)
        v_op(lambda e, o=stats_sb[:, 48:50], i=stats_sb[:, 0:nst * 6]: e.bn_aggr(o, i))
        p.wait("scalar", sV)
        a_op(lambda e, i=var, o=sd: e.activation(o, i, AF.Sqrt, bias=LN_EPS, scale=1.0))
        p.wait("vector", sA)
        v_op(lambda e, i=sd, o=rstd: e.reciprocal(o, i))
        p.wait("vector", sV)
        v_op(lambda e, o=nmb, a=mean, b=rstd: e.scalar_tensor_tensor(o, a, -1.0, b, ALU.mult, ALU.mult))
        p.wait("vector", sV)
        return rstd, nmb

    deq_tasks = []
    tr_tasks = {0: [], 1: [], 2: []}
    deq_st = {"v_done": 0, "store": {}, "cb": 0, "buf_guard": {0: 0, 1: 0}, "tpar": 0,
              "tr_guard": {}, "tq": 0}

    def _deq_chunk(codes_ap, scales_ap, dst_ap, fd, nblk, last_cb_of, mat):
        cell = {}

        def sub_dma():
            cb = deq_st["cb"]
            deq_st["cb"] ^= 1
            cell["cb"] = cb
            p.wait("scalar", sV, deq_st["buf_guard"][cb])
            p.dma("scalar", codes_sb[cb][:, 0:fd], codes_ap, sC2)
            p.dma("scalar", scl_sb[:, 0:nblk], scales_ap, sC2)
            cell["ready"] = sC2.n

        def mk_quad(k0):
            def sub(k0=k0):
                cb = cell["cb"]
                if k0 == 0:
                    p.wait("vector", sC2, cell["ready"])
                    p.wait("vector", sC2, deq_st.get("wrow_store", 0))
                for k in range(k0, k0 + 4):
                    if k == 0:
                        v_op(lambda e, o=acc_sb[:, 0:fd], i=codes_sb[cb][:, 0:fd],
                             kk=float(k), vv=float(NF4[k]):
                             e.tensor_scalar(o, i, kk, vv, ALU.is_equal, ALU.mult))
                    else:
                        v_op(lambda e, o=wrow_sb[:, 0:fd], i=codes_sb[cb][:, 0:fd],
                             kk=float(k), vv=float(NF4[k]):
                             e.tensor_scalar(o, i, kk, vv, ALU.is_equal, ALU.mult))
                        v_op(lambda e, o=acc_sb[:, 0:fd], a=acc_sb[:, 0:fd], b=wrow_sb[:, 0:fd]:
                             e.tensor_tensor(o, a, b, ALU.add))
            return sub

        def sub_fin():
            cb = cell["cb"]
            v_op(lambda e, o=scexp_sb[:, 0:fd].rearrange("p (b j) -> p b j", j=64),
                 i=scl_sb[:, 0:nblk].unsqueeze(2).broadcast_to([128, nblk, 64]): e.tensor_copy(o, i))
            p.wait("vector", sV)
            v_op(lambda e, o=wrow_sb[:, 0:fd], a=acc_sb[:, 0:fd], b=scexp_sb[:, 0:fd]:
                 e.tensor_tensor(o, a, b, ALU.mult))
            deq_st["buf_guard"][0] = sV.n
            deq_st["buf_guard"][1] = sV.n
            p.wait("scalar", sV)
            p.dma("scalar", dst_ap, wrow_sb[:, 0:fd], sC2)
            deq_st["wrow_store"] = sC2.n
            if last_cb_of is not None:
                deq_st["store"][last_cb_of] = sC2.n
        return [sub_dma, mk_quad(0), mk_quad(4), mk_quad(8), mk_quad(12), sub_fin]

    def _tr_pair(src_view, stage_pair, dst_view, mat, stage_key):
        def task():
            qi = 0
            q = "scalar"
            sem = sC2
            stage_sb = stage_pair[qi]
            p.wait(q, sC2, deq_st["store"][mat])
            gk = (qi, stage_key)
            p.wait(q, sem, deq_st["tr_guard"].get(gk, 0))
            p.dma(q, stage_sb, src_view, sem, transpose=True)
            p.wait(q, sem)
            p.dma(q, dst_view, stage_sb, sem)
            deq_st["tr_guard"][gk] = sem.n
        return task

    def build_deq_tasks():
        for mat, (codes_h_, scales_h_, dst_) in enumerate(
                ((gc_h, gs_h, wg_d), (uc_h, us_h, wu_d))):
            for ch in range(NIC):
                for hlf in range(2):
                    fd = 2048
                    last = mat if (ch == NIC - 1 and hlf == 1) else None
                    deq_tasks.extend(_deq_chunk(
                        codes_h_[ch * 128:(ch + 1) * 128, hlf * 2048:(hlf + 1) * 2048],
                        scales_h_[ch * 128:(ch + 1) * 128, hlf * 32:(hlf + 1) * 32],
                        dst_.ap()[ch * 128:(ch + 1) * 128, hlf * 2048:(hlf + 1) * 2048],
                        fd, 32, last, mat))
        for ch in range(DCH):
            last = 2 if ch == DCH - 1 else None
            deq_tasks.extend(_deq_chunk(
                dc_h[ch * 128:(ch + 1) * 128, :], ds_h[ch * 128:(ch + 1) * 128, :],
                wd_d.ap()[ch * 128:(ch + 1) * 128, :], NI, NI // 64, last, 2))
        wtst_pair = [wtst_sb[0][:, :], wtst_sb[1][:, :]]
        wdst_pair = [wdst_sb[0][:, :], wdst_sb[1][:, :]]
        for mat, (src, dst) in enumerate(((wg_d, wgT), (wu_d, wuT))):
            for hc in range(DCH):
                tr_tasks[mat].append(_tr_pair(src.ap()[:, hc * 128:(hc + 1) * 128],
                                              wtst_pair,
                                              dst.ap()[hc * 128:(hc + 1) * 128, :], mat, 0))
        for ic in range(NIC):
            for hf in range(2):
                tr_tasks[2].append(_tr_pair(
                    wd_d.ap()[hf * (D // 2):(hf + 1) * (D // 2), ic * 128:(ic + 1) * 128],
                    wdst_pair,
                    wdT.ap()[ic * 128:(ic + 1) * 128, hf * (D // 2):(hf + 1) * (D // 2)],
                    2, "wd"))

    def drip(n, trn=0):
        for _ in range(n):
            if deq_tasks:
                deq_tasks.pop(0)()
        for _ in range(trn):
            for mat in (0, 1, 2):
                if mat in deq_st["store"] and tr_tasks[mat]:
                    tr_tasks[mat].pop(0)()
                    break

    def tr_batch8(tiles, dst_flat, bank_idx, wait_v=None):
        tbk = tb[bank_idx]
        p.wait("tensor", sV, wait_v if wait_v is not None else sV.n)
        for j, t_ap in enumerate(tiles):
            t_tr(tbk[:, j * 128:(j + 1) * 128], t_ap, inc=(j == len(tiles) - 1))
        p.wait("vector", sPE)
        v_op(lambda e, o=dst_flat, i=tbk[:, 0:len(tiles) * 128]: e.tensor_copy(o, i))

    p.dma("sync", consts_sb[:, 0:len(cns["cvals"])], consts_d.ap(), sLD)
    p.dma("sync", ones_sb[:, 0:1], ones_bf_d.ap(), sLD)
    p.dma("sync", ident_bf[:, :], ident_bf_d.ap(), sLD)
    p.dma("sync", ident_f32[:, :], ident_f32_d.ap(), sLD)
    p.dma("sync", masks_sb[:, :, :], masks_d.ap().rearrange("p (j q) -> p j q", j=4), sLD)
    p.dma("sync", cos_sb[:, :, :], cos_d.ap().rearrange("(j p) f -> p j f", p=128), sLD)
    p.dma("sync", sin_sb[:, :, :], sin_d.ap().rearrange("(j p) f -> p j f", p=128), sLD)
    for w_sb, w_h in ((wq_sb, wq_h), (wk_sb, wk_h), (wv_sb, wv_h)):
        p.dma("gpsimd", w_sb[:, :, :], w_h.rearrange("(c p) o -> p c o", p=128), sGL)
    ph0_ld = sLD.n
    build_deq_tasks()
    p.wait("scalar", sLD, ph0_ld)
    a_op(lambda e: e.activation(stats_sb[:, 8:9], consts_sb[:, 0:1], AF.Exp))
    a_op(lambda e: e.activation(stats_sb[:, 8:9], consts_sb[:, 0:1], AF.Silu))

    ag0_base = sCC.n
    for pc in range(cfg.NPIECE):
        p.wait("sync", sLD, ph0_ld)
        p.wait("sync", sV)
        p.wait("sync", sA)
        p.dma("sync", xt1_sb[:, :], x_in[pc * 128:(pc + 1) * 128, :], sLD)
        p.wait("scalar", sLD)
        p.wait("vector", sLD)
        rstd, nmb = ln_stats(xt1_sb[:, :], D)
        p.wait("vector", sA)
        p.wait("vector", sST)
        v_op(lambda e, o=lnt1_sb[:, :], i=xt1_sb[:, :], a=rstd, b=nmb:
             e.tensor_scalar(o, i, a, b, ALU.mult, ALU.add))
        for b in range(DCH // 8):
            tiles = [lnt1_sb[:, (b * 8 + j) * 128:(b * 8 + j + 1) * 128] for j in range(8)]
            tr_batch8(tiles, lfm1_sb[:, b * 8:(b + 1) * 8, :].rearrange("p c t -> p (c t)"), b % 2)
        p.wait("sync", sV)
        p.dma("sync", ag0_in.ap()[pc].rearrange("(c p) t -> p c t", p=128),
              lfm1_sb[:, :, :], sST)
        if pc % 2 == 1:
            p.wait("gpsimd", sST)
            p.op("gpsimd", lambda e, pc=pc: e.collective_compute(
                "AllGather", ALU.bypass, replica_groups=rg,
                ins=[ag0_in.ap()[pc - 1:pc + 1].opt()],
                outs=[ag0_out.ap()[pc // 2].opt()]), sCC)

    p.wait("tensor", sGL)
    heads_rot = ((0, 0), (1, 2))
    bank_ev = {}
    for tt in range(cfg.NTT):
        r, c0 = (tt * 512) // TPC, (tt * 512) % TPC
        p.wait("sync", sCC, ag0_base + c0 // 256 + 2)
        p.wait("sync", sPE)
        p.wait("sync", sV)
        p.wait("sync", sA)
        for a_ in range(4):
            pgl = c0 // 128 + a_
            p.dma("sync", lnx_sb[:, :, a_ * 128:(a_ + 1) * 128],
                  ag0_out.ap()[pgl // 2, r, (pgl % 2) * D:(pgl % 2) * D + D, :]
                  .rearrange("(c p) t -> p c t", p=128), sLD)
        p.wait("tensor", sLD)
        p.wait("vector", sST)
        for proj, (w_sb, st_sb) in enumerate(((wq_sb, qst_sb), (wk_sb, kst_sb), (wv_sb, vst_sb))):
            for m in range(4):
                bi = (proj * 4 + m) % 6
                bank = pb[bi]
                p.wait("tensor", sV, bank_ev.get(bi, 0))
                for k in range(DCH):
                    t_mm(bank[:, :], w_sb[:, k, m * 128:(m + 1) * 128], lnx_sb[:, k, :],
                         start=(k == 0), stop=(k == DCH - 1), inc=(k == DCH - 1))
                p.wait("vector", sPE)
                v_op(lambda e, o=st_sb[:, m, :], i=bank[:, :]: e.tensor_copy(o, i))
                bank_ev[bi] = sV.n
                if (proj * 4 + m) % 2 == 0:
                    drip(1)
        st_ev = sV.n
        for st_sb in (qst_sb, kst_sb):
            for (e_h, m) in heads_rot:
                p.wait("tensor", sV, st_ev)
                for sub in range(4):
                    t_tr(tb[0][:, sub * 64:(sub + 1) * 64],
                         st_sb[0:64, m, sub * 128:(sub + 1) * 128], inc=(sub == 3))
                p.wait("vector", sPE)
                v_op(lambda e, o=rin_sb[:, :, :].rearrange("p c t -> p (c t)"), i=tb[0][:, 0:256]:
                     e.tensor_copy(o, i))
                cs = cos_sb[:, tt * 4:(tt + 1) * 4, :]
                sn = sin_sb[:, tt * 4:(tt + 1) * 4, :]
                ri = rin_sb[:, :, :].rearrange("p c (i two) -> p c i two", two=2)
                ro = rout_sb[:, :, :].rearrange("p c (i two) -> p c i two", two=2)
                ev, od = ri[:, :, :, 0], ri[:, :, :, 1]
                oe, oo = ro[:, :, :, 0], ro[:, :, :, 1]
                rt = rtmp_sb[:, :, :]
                v_op(lambda e, o=oe, a=ev, b=cs: e.tensor_tensor(o, a, b, ALU.mult))
                v_op(lambda e, o=rt, a=od, b=sn: e.tensor_tensor(o, a, b, ALU.mult))
                p.wait("vector", sV)
                v_op(lambda e, o=oe, a=oe, b=rt: e.tensor_tensor(o, a, b, ALU.add))
                v_op(lambda e, o=oo, a=od, b=cs: e.tensor_tensor(o, a, b, ALU.mult))
                p.wait("vector", sV)
                v_op(lambda e, o=rt, a=ev, b=sn: e.tensor_tensor(o, a, b, ALU.mult))
                p.wait("vector", sV)
                v_op(lambda e, o=oo, a=oo, b=rt: e.tensor_tensor(o, a, b, ALU.add))
                p.wait("vector", sV)
                p.wait("tensor", sV)
                for sub in range(4):
                    t_tr(tb[1][0:64, sub * 128:(sub + 1) * 128], rout_sb[:, sub, :], inc=(sub == 3))
                p.wait("vector", sPE)
                v_op(lambda e, o=st_sb[0:64, m, :], i=tb[1][0:64, 0:512]: e.tensor_copy(o, i))
                drip(1)
        for half in range(2):
            tiles = []
            for j in range(8):
                sub, m = half * 2 + j // 4, j % 4
                tiles.append(vst_sb[:, m, sub * 128:(sub + 1) * 128])
            tr_batch8(tiles, vtm_sb[:, half * 2:(half + 1) * 2, :, :].rearrange("p a m x -> p (a m x)"),
                      half, wait_v=st_ev)
        p.wait("sync", sV)
        p.dma("sync", q_sp.ap()[:, tt * 512:(tt + 1) * 512].rearrange("(m p) t -> p m t", p=128),
              qst_sb[:, :, :], sST)
        p.dma("sync", k_sp.ap()[:, tt * 512:(tt + 1) * 512].rearrange("(m p) t -> p m t", p=128),
              kst_sb[:, :, :], sST)
        p.dma("sync", v_sp.ap()[tt * 512:(tt + 1) * 512, :].rearrange("(s p) f -> p s f", p=128),
              vtm_sb[:, :, :, :].rearrange("p s m x -> p s (m x)"), sST)

    s_dbg = None
    if "sdbg" in debug_taps:
        s_dbg = nc.dram_tensor("dbg_srow", [2 * cfg.NQT, 1024], F32, kind="ExternalOutput").ap()
    for e_h in range(2):
        if e_h == 0:
            p.barrier()
        else:
            for eng in ("sync", "tensor", "vector", "scalar"):
                p.wait(eng, sPE)
                p.wait(eng, sV)
                p.wait(eng, sA)
        p.dma("sync", kh_sb[:, :, :],
              k_sp.ap()[e_h * 256:(e_h + 1) * 256, :].rearrange("(c p) t -> p c t", p=128), sLD)
        p.dma("sync", vh_sb[:, :, :],
              v_sp.ap()[:, e_h * 256:(e_h + 1) * 256].rearrange("(c p) f -> p c f", p=128), sLD)
        p.wait("tensor", sLD)
        probs_reuse = [0, 0, 0]
        score_reuse = [0, 0, 0]
        q_reuse = [0, 0]
        scm_used = [0, 0]
        scm_bufs = [scm_sb[:, :], srow_sb[:, 0:512]]
        recs_ev_prev = [0]
        sbank = {0: pb[0], 1: pb[1], 2: pb[5]}
        for qt in range(cfg.NQT):
            drip(7)
            qb = qt % 2
            p.wait("sync", sPE, q_reuse[qb])
            p.dma("sync", qt_sb[qb][:, :, :],
                  q_sp.ap()[e_h * 256:(e_h + 1) * 256, qt * 512:(qt + 1) * 512]
                  .rearrange("(c p) t -> p c t", p=128), sLD)
            p.wait("tensor", sLD)
            n_kc = 4 * qt + 4
            exp_cnt = [0] * n_kc

            def emit_scores(kc):
                par = kc % 3
                bank = sbank[par]
                if score_reuse[par]:
                    p.wait("tensor", sA, score_reuse[par])
                t_mm(bank[:, :], kh_sb[:, 0, kc * 128:(kc + 1) * 128], qt_sb[qb][:, 0, :],
                     start=True, stop=False, inc=False)
                t_mm(bank[:, :], kh_sb[:, 1, kc * 128:(kc + 1) * 128], qt_sb[qb][:, 1, :],
                     start=False, stop=True, inc=True)
                pe_after = sPE.n
                if probs_reuse[par]:
                    p.wait("scalar", sPE, probs_reuse[par])
                if kc >= 4 * qt:
                    off = kc - 4 * qt
                    sc2 = scm_bufs[kc % 2]
                    p.wait("vector", sPE, pe_after)
                    p.wait("vector", sA, scm_used[kc % 2])
                    v_op(lambda e, o=sc2, a=bank[:, :], b=masks_sb[:, off, :]:
                         e.tensor_tensor(o, a, b, ALU.add))
                    p.wait("scalar", sV)
                    a_op(lambda e, o=pT_sb[par][:, :], i=sc2:
                         e.activation(o, i, AF.Exp, bias=-EXP_SHIFT, scale=1.0 / 16.0))
                    scm_used[kc % 2] = sA.n
                else:
                    p.wait("scalar", sPE, pe_after)
                    a_op(lambda e, o=pT_sb[par][:, :], i=bank[:, :]:
                         e.activation(o, i, AF.Exp, bias=-EXP_SHIFT, scale=1.0 / 16.0))
                score_reuse[par] = sA.n
                exp_cnt[kc] = sA.n

            def emit_av(kc):
                par = kc % 3
                if kc == 0:
                    p.wait("tensor", sV, recs_ev_prev[0])
                p.wait("tensor", sA, exp_cnt[kc])
                st = (kc == 0)
                sp = (kc == n_kc - 1)
                t_mm(pb[2][:, :], vh_sb[:, kc, 0:128], pT_sb[par][:, :], start=st, stop=sp, inc=False)
                t_mm(pb[3][:, :], vh_sb[:, kc, 128:256], pT_sb[par][:, :], start=st, stop=sp, inc=False)
                t_mm(pb[4][0:1, :], ones_sb[:, 0:1], pT_sb[par][:, :], start=st, stop=sp, inc=True)
                probs_reuse[par] = sPE.n

            emit_scores(0)
            emit_scores(1)
            for kc in range(2, n_kc):
                emit_scores(kc)
                emit_av(kc - 2)
            emit_av(n_kc - 2)
            emit_av(n_kc - 1)
            q_reuse[qb] = sPE.n
            p.wait("vector", sPE)
            p.wait("vector", sA)
            p.wait("vector", sST)
            v_op(lambda e, o=avs_sb[:, 0, :], i=pb[2][:, :]: e.tensor_copy(o, i))
            v_op(lambda e, o=avs_sb[:, 1, :], i=pb[3][:, :]: e.tensor_copy(o, i))
            avs_ev = sV.n
            v_op(lambda e, o=srow_sb[0:1, 0:512], i=pb[4][0:1, :]: e.tensor_copy(o, i))
            p.wait("vector", sV)
            v_op(lambda e, o=srow_sb[0:1, 512:1024], i=srow_sb[0:1, 0:512]: e.reciprocal(o, i))
            p.wait("vector", sV)
            p.wait("tensor", sV)
            for sub in range(4):
                t_tr(pb[2][:, sub:sub + 1], srow_sb[0:1, 512 + sub * 128:512 + (sub + 1) * 128],
                     ident=ident_f32, inc=(sub == 3))
            p.wait("vector", sPE)
            v_op(lambda e, o=recs_sb[:, 0:4], i=pb[2][:, 0:4]: e.tensor_copy(o, i))
            recs_ev_prev[0] = sV.n
            p.wait("vector", sV)
            tiles = []
            for j in range(8):
                sub, half = j // 2, j % 2
                tiles.append(avs_sb[:, half, sub * 128:(sub + 1) * 128])
            tbk = tb[0]
            p.wait("tensor", sV, avs_ev)
            for j, t_ap in enumerate(tiles):
                t_tr(tbk[:, j * 128:(j + 1) * 128], t_ap, inc=(j == 7))
            p.wait("vector", sPE)
            for sub in range(4):
                v_op(lambda e, o=ctb_sb[:, sub, :], i=tbk[:, sub * 256:(sub + 1) * 256],
                     r=recs_sb[:, sub:sub + 1]: e.tensor_scalar(o, i, r, None, ALU.mult))
            p.wait("sync", sV)
            p.dma("sync", a2a_in[e_h].ap()[qt * 512:(qt + 1) * 512, :]
                  .rearrange("(s p) f -> p s f", p=128), ctb_sb[:, :, :], sST)
            if s_dbg is not None:
                row = e_h * cfg.NQT + qt
                p.dma("sync", s_dbg[row:row + 1, :], srow_sb[0:1, 0:1024], sST)
        p.wait("gpsimd", sST)
        p.op("gpsimd", lambda e, e_h=e_h: e.collective_compute(
            "AllToAll", ALU.bypass, replica_groups=rg,
            ins=[a2a_in[e_h].ap().opt()], outs=[a2a_out[e_h].ap().opt()]), sCC)

    p.barrier()
    for r in range(NC):
        drip(4, trn=4)
        rb = r % 2
        p.wait("sync", sPE)
        for e2 in range(2):
            p.dma("sync", avtm_sb[rb][:, :, e2 * 256:(e2 + 1) * 256],
                  a2a_out[e2].ap()[r * TPC:(r + 1) * TPC, :].rearrange("(s p) f -> p s f", p=128), sLD)
        p.wait("tensor", sLD)
        n_t = TPC // 128
        for b in range(max(1, n_t // 2)):
            ts_a = b * 2
            n_ts = min(2, n_t - ts_a)
            tiles = []
            for dts in range(n_ts):
                for fsub in range(4):
                    tiles.append(avtm_sb[rb][:, ts_a + dts, fsub * 128:(fsub + 1) * 128])
            tbk = tb[b % 2]
            p.wait("tensor", sV)
            for j, t_ap in enumerate(tiles):
                t_tr(tbk[:, j * 128:(j + 1) * 128], t_ap, inc=(j == len(tiles) - 1))
            p.wait("vector", sPE)
            for dts in range(n_ts):
                v_op(lambda e, i=tbk[:, dts * 512:(dts + 1) * 512].rearrange("p (c t) -> p c t", c=4),
                     o=avfm_sb[:, r * 4:(r + 1) * 4, (ts_a + dts) * 128:(ts_a + dts + 1) * 128]:
                     e.tensor_copy(o, i))
    avfm_done = sV.n
    y1_ev = {}
    for g in range(D // 256):
        drip(4, trn=4)
        gb = g % 2
        p.wait("gpsimd", sPE)
        p.dma("gpsimd", wo_sb[gb][:, :, :],
              wo_h[:, g * 256:(g + 1) * 256].rearrange("(c p) o -> p c o", p=128), sGL)
        p.wait("tensor", sGL)
        p.wait("tensor", sV, max(avfm_done, y1_ev.get(g % 2, 0)))
        for oc in range(2):
            for k in range(DCH):
                for t2 in range(NT2):
                    t_mm(pb[oc * NT2 + t2][:, :], wo_sb[gb][:, k, oc * 128:(oc + 1) * 128],
                         avfm_sb[:, k, t2 * 512:(t2 + 1) * 512],
                         start=(k == 0), stop=(k == DCH - 1),
                         inc=(k == DCH - 1 and t2 == NT2 - 1))
        p.wait("vector", sPE)
        p.wait("vector", sST)
        for oc in range(2):
            for t2 in range(NT2):
                v_op(lambda e, o=y1f_sb[:, oc, t2 * 512:(t2 + 1) * 512], i=pb[oc * NT2 + t2][:, :]:
                     e.tensor_copy(o, i))
        y1_ev[g % 2] = sV.n
        y1f_ev = sV.n
        n_t = TPC // 128
        for b in range(max(1, n_t // 4)):
            ts0 = b * 4
            n_ts = min(4, n_t - ts0)
            tiles = []
            for dts in range(n_ts):
                for oc in range(2):
                    tiles.append(y1f_sb[:, oc, (ts0 + dts) * 128:(ts0 + dts + 1) * 128])
            tr_batch8(tiles, y1s_sb[:, ts0:ts0 + n_ts, :].rearrange("p s f -> p (s f)"), b % 2,
                      wait_v=y1f_ev)
        p.wait("sync", sV)
        p.dma("sync", y1_sp.ap()[:, g * 256:(g + 1) * 256].rearrange("(s p) f -> p s f", p=128),
              y1s_sb[:, :, :], sST)

    p.barrier()
    for pc in range(cfg.NPIECE):
        drip(8, trn=6)
        p.wait("sync", sV)
        p.wait("sync", sA)
        p.wait("sync", sST)
        p.dma("sync", y1t_sb[:, :], y1_sp.ap()[pc * 128:(pc + 1) * 128, :], sLD)
        p.dma("sync", xt1_sb[:, :], x_in[pc * 128:(pc + 1) * 128, :], sLD)
        p.wait("scalar", sLD)
        p.wait("vector", sLD)
        rstd, nmb = ln_stats(y1t_sb[:, :], D)
        p.wait("vector", sA)
        v_op(lambda e, o=xsq1_sb[:, :], i=y1t_sb[:, :], a=rstd, b=nmb:
             e.tensor_scalar(o, i, a, b, ALU.mult, ALU.add))
        p.wait("vector", sV)
        v_op(lambda e, o=xp_sb[:, :], a=xt1_sb[:, :], b=xsq1_sb[:, :]:
             e.tensor_tensor(o, a, b, ALU.add))
        p.wait("sync", sV)
        p.dma("sync", xp_sp.ap()[pc * 128:(pc + 1) * 128, :], xp_sb[:, :], sST)
        rstd, nmb = ln_stats(xp_sb[:, :], D)
        p.wait("vector", sA)
        p.wait("vector", sST)
        v_op(lambda e, o=lnt1_sb[:, :], i=xp_sb[:, :], a=rstd, b=nmb:
             e.tensor_scalar(o, i, a, b, ALU.mult, ALU.add))
        for b in range(DCH // 8):
            tiles = [lnt1_sb[:, (b * 8 + j) * 128:(b * 8 + j + 1) * 128] for j in range(8)]
            tr_batch8(tiles, lfm1_sb[:, b * 8:(b + 1) * 8, :].rearrange("p c t -> p (c t)"), b % 2)
        p.wait("sync", sV)
        p.dma("sync", ag2_in.ap()[pc].rearrange("(c p) t -> p c t", p=128),
              lfm1_sb[:, :, :], sST)
        if pc % 2 == 1:
            p.wait("gpsimd", sST)
            p.op("gpsimd", lambda e, pc=pc: e.collective_compute(
                "AllGather", ALU.bypass, replica_groups=rg,
                ins=[ag2_in.ap()[pc - 1:pc + 1].opt()],
                outs=[ag2_out.ap()[pc // 2].opt()]), sCC)

    while deq_tasks:
        deq_tasks.pop(0)()
    for mat in (0, 1, 2):
        while tr_tasks[mat]:
            tr_tasks[mat].pop(0)()

    p.barrier()
    halves = [list(range(0, half1)), list(range(half1, NIC))]
    for d_blk in range(NC):
        p.wait("sync", sPE)
        p.wait("sync", sV)
        p.wait("sync", sST)
        p.wait("vector", sPE)
        for a_ in range(NP):
            p.dma("sync", lnx2_sb[:, :, a_ * 128:(a_ + 1) * 128],
                  ag2_out.ap()[a_ // 2, d_blk, (a_ % 2) * D:(a_ % 2) * D + D, :]
                  .rearrange("(c p) t -> p c t", p=128), sLD)
        for half in halves:
            nm = len(half)
            m0 = half[0]
            p.wait("sync", sPE)
            p.dma("sync", whg_sb[:, :, 0:nm * 128],
                  wgT.ap()[:, m0 * 128:(m0 + nm) * 128].rearrange("(c p) i -> p c i", p=128), sLD)
            p.dma("sync", whu_sb[:, :, 0:nm * 128],
                  wuT.ap()[:, m0 * 128:(m0 + nm) * 128].rearrange("(c p) i -> p c i", p=128), sLD)
            p.wait("tensor", sLD)
            for ml in range(nm):
                m = m0 + ml
                p.wait("tensor", sV)
                p.wait("tensor", sA)
                for proj, wh in ((0, whg_sb), (1, whu_sb)):
                    for k in range(DCH):
                        for t2 in range(NT2):
                            t_mm(pb[proj * NT2 + t2][:, :], wh[:, k, ml * 128:(ml + 1) * 128],
                                 lnx2_sb[:, k, t2 * 512:(t2 + 1) * 512],
                                 start=(k == 0), stop=(k == DCH - 1),
                                 inc=(proj == 1 and k == DCH - 1 and t2 == NT2 - 1))
                for t2 in range(NT2):
                    p.wait("scalar", sPE)
                    p.wait("scalar", sV)
                    a_op(lambda e, o=gsl_sb[:, :], i=pb[t2][:, :]: e.activation(o, i, AF.Silu))
                    p.wait("vector", sA)
                    v_op(lambda e, o=inter_sb[:, m, t2 * 512:(t2 + 1) * 512], a=gsl_sb[:, :],
                         b=pb[NT2 + t2][:, :]: e.tensor_tensor(o, a, b, ALU.mult))
        p.wait("sync", sPE)
        p.dma("sync", wdt_sb[:, :, :], wdT.ap().rearrange("(c p) h -> p c h", p=128), sLD)
        p.wait("tensor", sLD)
        for hc in range(DCH):
            banks = [pb[4 + t2] if hc % 2 == 0 else pb[0 + t2] for t2 in range(NT2)]
            p.wait("tensor", sV)
            for m in range(NIC):
                for t2 in range(NT2):
                    t_mm(banks[t2][:, :], wdt_sb[:, m, hc * 128:(hc + 1) * 128],
                         inter_sb[:, m, t2 * 512:(t2 + 1) * 512],
                         start=(m == 0), stop=(m == NIC - 1),
                         inc=(m == NIC - 1 and t2 == NT2 - 1))
            p.wait("vector", sPE)
            p.wait("vector", sST)
            for t2 in range(NT2):
                v_op(lambda e, o=dstg_sb[:, hc, t2 * 512:(t2 + 1) * 512], i=banks[t2][:, :]:
                     e.tensor_copy(o, i))
        p.wait("sync", sV)
        p.dma("sync", rs_in.ap()[d_blk * D:(d_blk + 1) * D, :]
              .rearrange("(c p) t -> p c t", p=128), dstg_sb[:, :, :], sST)
    p.wait("gpsimd", sST)
    p.op("gpsimd", lambda e: e.collective_compute(
        "ReduceScatter", ALU.add, replica_groups=rg,
        ins=[rs_in.ap().opt()], outs=[rs_out.ap().opt()]), sCC)

    p.barrier()
    for pc in range(cfg.NPIECE):
        fb = pc % 2
        p.wait("sync", sPE)
        p.wait("sync", sV)
        p.wait("sync", sST)
        p.dma("sync", fml2_sb[fb][:, :, :],
              rs_out.ap()[:, pc * 128:(pc + 1) * 128].rearrange("(c p) t -> p c t", p=128), sLD)
        p.dma("sync", xp_sb[:, :], xp_sp.ap()[pc * 128:(pc + 1) * 128, :], sLD)
        p.wait("tensor", sLD)
        for b in range(DCH // 8):
            tiles = [fml2_sb[fb][:, b * 8 + j, :] for j in range(8)]
            tbk = tb[b % 2]
            p.wait("tensor", sV)
            for j, t_ap in enumerate(tiles):
                t_tr(tbk[:, j * 128:(j + 1) * 128], t_ap, inc=(j == 7))
            p.wait("vector", sPE)
            v_op(lambda e, o=y2t_sb[:, b * 8 * 128:(b + 1) * 8 * 128], i=tbk[:, 0:1024]:
                 e.tensor_copy(o, i))
        rstd, nmb = ln_stats(y2t_sb[:, :], D)
        p.wait("vector", sA)
        v_op(lambda e, o=xsq1_sb[:, :], i=y2t_sb[:, :], a=rstd, b=nmb:
             e.tensor_scalar(o, i, a, b, ALU.mult, ALU.add))
        p.wait("vector", sV)
        v_op(lambda e, o=xp_sb[:, :], a=xp_sb[:, :], b=xsq1_sb[:, :]:
             e.tensor_tensor(o, a, b, ALU.add))
        p.wait("sync", sV)
        p.dma("sync", out_h[pc * 128:(pc + 1) * 128, :], xp_sb[:, :], sST)
    p.wait("sync", sST)
    if debug_taps:
        scratch = {"ag0_in": ag0_in, "ag0_out": ag0_out, "q_sp": q_sp, "k_sp": k_sp,
                   "v_sp": v_sp, "a2a_in": a2a_in, "a2a_out": a2a_out, "y1_sp": y1_sp,
                   "xp_sp": xp_sp, "ag2_in": ag2_in, "ag2_out": ag2_out, "wg_d": wg_d,
                   "wgT": wgT, "wdT": wdT, "ar_in": rs_in, "ar_out": rs_out}
        for nm in debug_taps:
            if nm == "sdbg":
                continue
            t = scratch[nm]
            dbg = nc.dram_tensor("dbg_" + nm, list(t.ap().shape), t.ap().dtype,
                                 kind="ExternalOutput").ap()
            p.dma("sync", dbg, t.ap(), sST)
        p.wait("sync", sST)
    p.barrier()

    with nc.Block() as block:
        with allow_ici_all_to_all():
            for eng_name, attr in (("gpsimd", block.gpsimd), ("scalar", block.scalar),
                                   ("tensor", block.tensor), ("vector", block.vector),
                                   ("sync", block.sync)):
                def body(e, cmds=p.cmds[eng_name]):
                    for fn in cmds:
                        fn(e)
                attr(body)
    return nc



def shard_inputs(cfg, a):
    NC = cfg.NCORES
    x = np.asarray(a["L0_attn_hidden_states"], np.float32).reshape(cfg.S, cfg.D)
    wq = np.asarray(a["L0_attn_Wq"], np.float32)
    wk = np.asarray(a["L0_attn_Wk"], np.float32)
    wv = np.asarray(a["L0_attn_Wv"], np.float32)
    wo = np.asarray(a["L0_attn_Wo"], np.float32)
    gc = np.asarray(a["L0_ffn_gate_codes"], np.int32)
    gs = np.asarray(a["L0_ffn_gate_scales"], np.float32)
    uc = np.asarray(a["L0_ffn_up_codes"], np.int32)
    us = np.asarray(a["L0_ffn_up_scales"], np.float32)
    dc = np.asarray(a["L0_ffn_down_codes"], np.int32)
    ds = np.asarray(a["L0_ffn_down_scales"], np.float32)
    counts = cfg.I_COUNTS
    starts = np.cumsum([0] + counts[:-1]) * 128
    NI = cfg.NI
    in_maps = []
    for c in range(NC):
        s0 = int(starts[c])
        n = counts[c] * 128

        def padr(m):
            if m.shape[0] == NI:
                return m
            return np.concatenate(
                [m, np.zeros((NI - m.shape[0],) + m.shape[1:], m.dtype)], axis=0)

        def padc(m, cols):
            if m.shape[1] == cols:
                return m
            return np.concatenate(
                [m, np.zeros((m.shape[0], cols - m.shape[1]), m.dtype)], axis=1)

        in_maps.append({
            "x": np.ascontiguousarray(x[c * cfg.TPC:(c + 1) * cfg.TPC, :]),
            "wq": np.ascontiguousarray(wq[:, c * 512:(c + 1) * 512]),
            "wk": np.ascontiguousarray(wk[:, c * 512:(c + 1) * 512]),
            "wv": np.ascontiguousarray(wv[:, c * 512:(c + 1) * 512]),
            "wo": np.ascontiguousarray(wo),
            "gc": padr(np.ascontiguousarray(gc[s0:s0 + n, :])),
            "gs": padr(np.ascontiguousarray(gs[s0:s0 + n, :])),
            "uc": padr(np.ascontiguousarray(uc[s0:s0 + n, :])),
            "us": padr(np.ascontiguousarray(us[s0:s0 + n, :])),
            "dc": padc(np.ascontiguousarray(dc[:, s0:s0 + n]), NI),
            "ds": padc(np.ascontiguousarray(ds[:, s0 // 64:(s0 + n) // 64]), NI // 64),
        })
    return in_maps


def kernel(**inputs):
    cfg = CFG()
    in_maps = shard_inputs(cfg, inputs)
    nc = build_nc(cfg)
    res = run_bass_kernel_spmd(nc, in_maps, core_ids=list(range(cfg.NCORES)))
    out = np.concatenate([np.asarray(r["out"]) for r in res.results], axis=0)
    return out.reshape(1, cfg.S, cfg.D).astype(np.float32)
